# version 20
# baseline (speedup 1.0000x reference)
import sys
if '/opt/trn_rl_repo' not in sys.path:
    sys.path.insert(0, '/opt/trn_rl_repo')

import numpy as np
import ml_dtypes
import concourse.bass as bass
import concourse.mybir as mybir
from concourse.bass_utils import run_bass_kernel_spmd
from contextlib import ExitStack

F32 = mybir.dt.float32
BF16 = mybir.dt.float16
U16 = mybir.dt.uint16
AX = mybir.AxisListType.X
OP = mybir.AluOpType
AF = mybir.ActivationFunctionType

S = 2048
D = 2048
H = 16
G = 2
HD = 128
I = 8192
NQT = 4
QL = 512
EPS = 1e-5

ENGINES = ("pe", "act", "dve", "pool", "sync")


class Prog:

    def __init__(self):
        self.ops = {e: [] for e in ENGINES}
        self.cnt = {c: 0 for c in ENGINES}
        self.tok = {}
        self.marked = {c: set() for c in ENGINES}
        self.sat = {e: {} for e in ENGINES}
        self.lanes = []

    def _dep(self, eng, waits, chan, idx):
        if chan == 'pe' and eng == 'pe':
            return
        if idx <= self.sat[eng].get(chan, -1):
            return
        if waits.get(chan, -1) < idx:
            waits[chan] = idx

    def op(self, eng, fn, reads=(), writes=(), dma=False, lane=None, tag=""):
        if dma:
            chan = 'dl_' + lane
            if chan not in self.cnt:
                self.cnt[chan] = 0
                self.lanes.append(chan)
        else:
            chan = eng
        waits = {}
        for t in reads:
            st = self.tok.get(t)
            if st and st['w'] is not None:
                self._dep(eng, waits, *st['w'])
        for t in writes:
            st = self.tok.get(t)
            if st:
                if st['w'] is not None:
                    self._dep(eng, waits, *st['w'])
                for (c, i) in st['r']:
                    self._dep(eng, waits, c, i)
        idx = self.cnt[chan]
        self.cnt[chan] += 1
        self.ops[eng].append(dict(fn=fn, waits=waits, chan=chan, idx=idx, dma=dma, tag=tag))
        for c, i in waits.items():
            self.sat[eng][c] = max(self.sat[eng].get(c, -1), i)
            if not c.startswith('dl_'):
                self.marked[c].add(i)
        me = (chan, idx)
        for t in reads:
            self.tok.setdefault(t, {'w': None, 'r': []})['r'].append(me)
        for t in writes:
            self.tok[t] = {'w': me, 'r': []}

    def finalize(self):
        self.prefix = {}
        for c in ENGINES:
            pos = {}
            for n, i in enumerate(sorted(self.marked[c])):
                pos[i] = n + 1
            self.prefix[c] = pos

    def val(self, chan, idx):
        if chan.startswith('dl_'):
            return 16 * (idx + 1)
        return self.prefix[chan][idx]

    def emit(self, eng, h, sems):
        for rec in self.ops[eng]:
            for chan in sorted(rec['waits']):
                h.wait_ge(sems[chan], self.val(chan, rec['waits'][chan]))
            ins = rec['fn'](h)
            if rec['dma']:
                ins.then_inc(sems[rec['chan']], 16)
            elif rec['idx'] in self.marked[rec['chan']]:
                ins.then_inc(sems[rec['chan']], 1)


def build_program():
    nc = bass.Bass("TRN2", target_bir_lowering=False)

    dx = nc.dram_tensor("x", [S, D], F32, kind="ExternalInput")
    dwq = nc.dram_tensor("wq_y", [H, 128, 16, 128], F32, kind="ExternalInput")
    dwk = nc.dram_tensor("wk", [D, G * HD], F32, kind="ExternalInput")
    dwv = nc.dram_tensor("wv", [D, G * HD], F32, kind="ExternalInput")
    dwo = nc.dram_tensor("wo", [D, D], F32, kind="ExternalInput")
    dwg = nc.dram_tensor("wg_y", [64, 128, 16, 128], F32, kind="ExternalInput")
    dwu = nc.dram_tensor("wu_y", [64, 128, 16, 128], F32, kind="ExternalInput")
    dwd = nc.dram_tensor("wd_p", [I, D], F32, kind="ExternalInput")
    dbq = nc.dram_tensor("bq_y", [128, H], F32, kind="ExternalInput")
    dbk = nc.dram_tensor("bk_y", [128, G], F32, kind="ExternalInput")
    dbv = nc.dram_tensor("bv_y", [128, H], F32, kind="ExternalInput")
    dbo = nc.dram_tensor("bo_r", [1, D], F32, kind="ExternalInput")
    drope = nc.dram_tensor("rope", [128, 2 * S], F32, kind="ExternalInput")
    dident = nc.dram_tensor("ident", [128, 128], F32, kind="ExternalInput")
    donesc = nc.dram_tensor("ones_c", [128, 1], F32, kind="ExternalInput")
    donesr = nc.dram_tensor("ones_r", [1, 128], F32, kind="ExternalInput")
    deps = nc.dram_tensor("eps_c", [128, 1], F32, kind="ExternalInput")
    dy = nc.dram_tensor("y", [S, D], F32, kind="ExternalOutput")

    es = ExitStack()
    sb = lambda name, shape, dt: es.enter_context(nc.sbuf_tensor(name, shape, dt))
    ps = lambda name, shape, dt: es.enter_context(nc.psum_tensor(name, shape, dt))

    XNT = sb("XNT", [128, 16 * 2048], BF16)
    KT = sb("KT", [128, G * 2048], BF16)
    VSB = sb("VSB", [128, 16 * 256], BF16)
    ACC = sb("ACC", [128, 8192], F32)
    QT = sb("QT", [128, 16 * 512], BF16)
    VAC = sb("VAC", [128, 16 * 512], BF16)
    SCR = sb("SCR", [128, 16 * 512], BF16)
    WSM = sb("WSM", [128, 3 * 2048], BF16)
    XNS = sb("XNS", [128, 2048], BF16)
    RS = sb("RS", [128, 1024], F32)
    SMALL = sb("SMALL", [128, 72], F32)
    BVBF = sb("BVBF", [128, 16], BF16)
    IDENT = sb("IDENT", [128, 128], F32)
    ONESC = sb("ONESC", [128, 1], BF16)
    ONESR = sb("ONESR", [1, 128], F32)
    NORMS = sb("NORMS", [1, 64], F32)
    RROW = sb("RROW", [1, 512], F32)
    BOR = sb("BOR", [1, 2048], F32)

    PP = [ps(f"PP{i}", [128, 1024], F32) for i in range(4)]

    def pphalf(i, half):
        return PP[i][:, half * 512:(half + 1) * 512], ('PB', i, half)

    p = Prog()
    sems = {}

    def mm(dst, lhsT, rhs, reads, wtok, start, stop, tag=""):
        p.op('pe', lambda h, a=dst, b=lhsT, c=rhs, s1=start, s2=stop:
             h.matmul(a, b, c, start=s1, stop=s2), reads=reads, writes=[wtok], tag=tag)

    def transpose(dst, src, reads, wtok, tag=""):
        p.op('pe', lambda h, a=dst, b=src: h.transpose(a, b, IDENT[:, :]),
             reads=list(reads) + ['IDENT'], writes=[wtok], tag=tag)

    def act(fn, reads, writes, tag=""):
        p.op('act', fn, reads=reads, writes=writes, tag=tag)

    def dve(fn, reads, writes, tag=""):
        p.op('dve', fn, reads=reads, writes=writes, tag=tag)

    def copy_any(eng, dst, src, reads, writes, tag=""):
        if eng == 'act':
            p.op('act', lambda h, o=dst, i_=src: h.copy(o, i_), reads=reads, writes=writes, tag=tag)
        else:
            p.op('dve', lambda h, o=dst, i_=src: h.tensor_copy(o, i_), reads=reads, writes=writes, tag=tag)

    def dma(eng, lane, dst, src, reads, writes, tag=""):
        p.op(eng, lambda h, a=dst, b=src: h.dma_start(a, b),
             reads=reads, writes=writes, dma=True, lane=lane, tag=tag)

    def xnt_toks(dc_list, s0, s1):
        t0, t1 = s0 // 128, (s1 + 127) // 128
        return [('XNT', dc, t) for dc in dc_list for t in range(t0, t1)]

    def xnt_ap(dc, s0, s1):
        return XNT[:, dc * 2048 + s0: dc * 2048 + s1]

    def acc_toks(c0, c1):
        return [('ACC', i) for i in range(c0 // 512, (c1 + 511) // 512)]

    def reg_toks(name, c0, c1):
        return [(name, i) for i in range(c0 // 512, (c1 + 511) // 512)]

    QTF = QT.ap().bitcast(F32)
    XNSF = XNS.ap().bitcast(F32)
    VACF = VAC.ap().bitcast(F32)
    SCRF = SCR.ap().bitcast(F32)
    RSBF = RS.ap().bitcast(BF16)

    dma('sync', 'cbq', SMALL[:, 0:16], dbq[:, :], [], [('SM', 'bq')])
    dma('sync', 'cbk', SMALL[:, 16:18], dbk[:, :], [], [('SM', 'bk')])
    dma('sync', 'ceps', SMALL[:, 34:35], deps[:, :], [], [('SM', 'eps')])
    dma('sync', 'cid', IDENT[:, :], dident[:, :], [], ['IDENT'])
    dma('sync', 'conr', ONESR[:, :], donesr[:, :], [], ['ONESR'])
    dma('sync', 'cbor', BOR[:, :], dbo[:, :], [], ['BOR'])
    dma('pool', 'conc', ONESC[:, :], donesc[:, :], [], ['ONESC'])
    dma('pool', 'cbv', BVBF[:, :], dbv[:, :], [], ['BVBF'])
    dma('pool', 'wsm0', WSM[:, 0:4096].rearrange("p (c f) -> p c f", c=16),
        dwk.ap().rearrange("(c p) f -> p c f", p=128), [], [('WSM', 0), ('WSM', 1)], tag="ldwk")
    dma('sync', 'crope', ACC[:, 0:4096], drope[:, :], [], acc_toks(0, 4096))

    def emit_qnorm(hh):
        qsl = 2 + (hh % 2)
        prow = PP[2][0:1, 0:512]
        mm(prow, ONESC[:, :], XNS[:, qsl * 512:(qsl + 1) * 512],
           ['ONESC', ('XNS', qsl)], ('PB', 2, 0), start=True, stop=True, tag=f"qnorm{hh}")
        dve(lambda h, o=NORMS[0:1, hh:hh + 1], i_=prow: h.reduce_max(o, i_, AX),
            [('PB', 2, 0)], [('NM', 'q', hh)])

    def emit_qbuild_head(q, h_):
        qs0 = q * QL
        if q == 0:
            cosl, sinl = ACC[:, 0:512], ACC[:, 2048:2560]
            ctoks = acc_toks(0, 512) + acc_toks(2048, 2560)
        else:
            cosl, sinl = XNSF[:, 0:512], XNSF[:, 512:1024]
            ctoks = [('XNS', i) for i in range(4)]
        sl = (h_ + 2) % 3
        wtok = ('WSM', sl)
        dma('pool', f"wsm{sl}", WSM[:, sl * 2048:(sl + 1) * 2048].rearrange("p (c f) -> p c f", c=16),
            dwq[h_], [], [wtok], tag=f"ldwq{q}_{h_}")
        pb, btok = pphalf(0, h_ % 2)
        for dc in range(16):
            mm(pb, WSM[:, sl * 2048 + dc * 128: sl * 2048 + (dc + 1) * 128],
               xnt_ap(dc, qs0, qs0 + 512),
               [wtok] + xnt_toks([dc], qs0, qs0 + 512),
               btok, start=(dc == 0), stop=(dc == 15), tag=f"q{q}_{h_}")
        act(lambda h, o=RS[:, 0:512], i_=pb, b=SMALL[:, h_:h_ + 1]:
            h.activation(o, i_, AF.Identity, bias=b), [btok, ('SM', 'bq')], ['RS'])
        dve(lambda h, o=RS[0:64, 512:1024], a=RS[64:128, 0:512], b=sinl[64:128, :]:
            h.tensor_tensor(o, a, b, OP.mult), ['RS'] + ctoks, ['RS'])
        dve(lambda h, o=RS[64:128, 512:1024], a=RS[0:64, 0:512], b=sinl[0:64, :]:
            h.tensor_tensor(o, a, b, OP.mult), ['RS'] + ctoks, ['RS'])
        dve(lambda h, o=RS[:, 0:512], a=RS[:, 0:512], b=cosl:
            h.tensor_tensor(o, a, b, OP.mult), ['RS'] + ctoks, ['RS'])
        qap = QT[:, h_ * 512:(h_ + 1) * 512]
        qtok = ('QT', h_)
        dve(lambda h, o=qap, a=RS[:, 0:512], b=RS[:, 512:1024]:
            h.tensor_tensor(o, a, b, OP.add), ['RS'], [qtok], tag=f"qrope{q}_{h_}")
        if q == 0:
            qsl = 2 + (h_ % 2)
            dve(lambda h, o=XNS[:, qsl * 512:(qsl + 1) * 512], a=qap:
                h.tensor_tensor(o, a, a, OP.mult), [qtok], [('XNS', qsl)])
            if h_ > 0:
                emit_qnorm(h_ - 1)

    KTF = KT.ap().bitcast(F32)
    VSBF = VSB.ap().bitcast(F32)

    def emit_ln_tile(t):
        sl = t % 4
        if sl == 0:
            xap, xtok, xlane = SCRF[:, 0:2048], reg_toks('SCR', 0, 4096), 'scrh0'
        elif sl == 1:
            xap, xtok, xlane = SCRF[:, 2048:4096], reg_toks('SCR', 4096, 8192), 'scrh1'
        elif sl == 2:
            xap, xtok, xlane = ACC[:, 4096:6144], acc_toks(4096, 6144), 'lnkt'
        else:
            xap, xtok, xlane = ACC[:, 6144:8192], acc_toks(6144, 8192), 'lnvs'
        if sl < 2:
            cap = QTF[:, sl * 2048:(sl + 1) * 2048]
            ctok = reg_toks('QT', sl * 4096, (sl + 1) * 4096)
        else:
            cap = VACF[:, (sl - 2) * 2048:(sl - 1) * 2048]
            ctok = reg_toks('VAC', (sl - 2) * 4096, (sl - 1) * 4096)
        dma('sync', xlane, xap, dx[t * 128:(t + 1) * 128, :], [], xtok, tag=f"ldx{t}")
        c0 = 36 + sl * 9
        s1 = SMALL[:, c0:c0 + 1]
        st1 = ('SM', 's1', sl)
        s2 = SMALL[:, c0 + 1:c0 + 2]
        t2 = ('SM', 's2', sl)
        xdump = [('XNS', i) for i in range(4)]
        dve(lambda h, o=s1, i_=xap: h.reduce_sum(o, i_, AX), xtok, [st1])
        dve(lambda h, o=XNSF[:, 0:1024], i0=xap[:, 0:1024], i1=xap[:, 0:1024], a=s2:
            h.scalar_tensor_tensor(o, i0, 1.0, i1, OP.mult, OP.mult, accum_out=a),
            xtok, xdump + [t2], tag=f"sqa{t}")
        s2b = SMALL[:, c0 + 2:c0 + 3]
        t2b = ('SM', 's2b', sl)
        dve(lambda h, o=XNSF[:, 0:1024], i0=xap[:, 1024:2048], i1=xap[:, 1024:2048], a=s2b:
            h.scalar_tensor_tensor(o, i0, 1.0, i1, OP.mult, OP.mult, accum_out=a),
            xtok, xdump + [t2b], tag=f"sqb{t}")
        m = SMALL[:, c0 + 3:c0 + 4]
        mt = ('SM', 'm', sl)
        dve(lambda h, o=m, i_=s1: h.tensor_single_scalar(o, i_, 1.0 / D, OP.mult), [st1], [mt])
        v1 = SMALL[:, c0 + 4:c0 + 5]
        vt1 = ('SM', 'v1', sl)
        dve(lambda h, o=v1, a=s2, b=s2b: h.tensor_tensor(o, a, b, OP.add), [t2, t2b], [vt1])
        msq = SMALL[:, c0 + 5:c0 + 6]
        msqt = ('SM', 'msq', sl)
        dve(lambda h, o=msq, i_=m: h.tensor_tensor(o, i_, i_, OP.mult), [mt], [msqt])
        dve(lambda h, o=v1, a=v1, s=1.0 / D, b=msq: h.scalar_tensor_tensor(o, a, s, b, OP.mult, OP.subtract),
            [vt1, msqt], [vt1])
        std = SMALL[:, c0 + 6:c0 + 7]
        stdt = ('SM', 'std', sl)
        act(lambda h, o=std, i_=v1: h.activation(o, i_, AF.Sqrt, bias=SMALL[:, 34:35]),
            [vt1, ('SM', 'eps')], [stdt])
        rstd = SMALL[:, c0 + 7:c0 + 8]
        rt = ('SM', 'rstd', sl)
        dve(lambda h, o=rstd, i_=std: h.reciprocal(o, i_), [stdt], [rt])
        dve(lambda h, o=cap, i_=xap, a=m, b=rstd: h.tensor_scalar(o, i_, a, b, OP.subtract, OP.mult),
            xtok + [mt, rt], ctok, tag=f"ln{t}")
        for grp in range(4):
            pb, btok = pphalf(2 + grp // 2, grp % 2)
            for k in range(4):
                dc = grp * 4 + k
                transpose(pb[:, k * 128:(k + 1) * 128], cap[:, dc * 128:(dc + 1) * 128],
                          ctok, btok, tag=f"tp{t}_{dc}")
            dst = XNT.ap().rearrange("p (c f) -> p c f", f=2048)[:, grp * 4:grp * 4 + 4,
                                                                t * 128:(t + 1) * 128]
            srcv = pb.rearrange("p (c f) -> p c f", f=128)
            copy_any('act', dst, srcv, [btok],
                     [('XNT', grp * 4 + k, t) for k in range(4)], tag=f"ev{t}_{grp}")

    def emit_k_sc(sc):
        for g in range(G):
            pb, btok = pphalf(0, (2 * sc + g) % 2)
            for dc in range(16):
                mm(pb, WSM[:, dc * 256 + g * 128: dc * 256 + (g + 1) * 128],
                   xnt_ap(dc, sc * 512, (sc + 1) * 512),
                   [('WSM', 0), ('WSM', 1)] + xnt_toks([dc], sc * 512, (sc + 1) * 512),
                   btok, start=(dc == 0), stop=(dc == 15), tag=f"k{g}{sc}")
            act(lambda h, o=RS[:, 0:512], i_=pb, b=SMALL[:, 16 + g:17 + g]:
                h.activation(o, i_, AF.Identity, bias=b), [btok, ('SM', 'bk')], ['RS'])
            cosl = ACC[:, sc * 512:(sc + 1) * 512]
            sinl = ACC[:, 2048 + sc * 512: 2048 + (sc + 1) * 512]
            ctoks = acc_toks(sc * 512, (sc + 1) * 512) + acc_toks(2048 + sc * 512,
                                                                  2048 + (sc + 1) * 512)
            dve(lambda h, o=RS[0:64, 512:1024], a=RS[64:128, 0:512], b=sinl[64:128, :]:
                h.tensor_tensor(o, a, b, OP.mult), ['RS'] + ctoks, ['RS'])
            dve(lambda h, o=RS[64:128, 512:1024], a=RS[0:64, 0:512], b=sinl[0:64, :]:
                h.tensor_tensor(o, a, b, OP.mult), ['RS'] + ctoks, ['RS'])
            dve(lambda h, o=RS[:, 0:512], a=RS[:, 0:512], b=cosl:
                h.tensor_tensor(o, a, b, OP.mult), ['RS'] + ctoks, ['RS'])
            kap = KT[:, g * 2048 + sc * 512: g * 2048 + (sc + 1) * 512]
            ktok = ('KT', g * 4 + sc)
            dve(lambda h, o=kap, a=RS[:, 0:512], b=RS[:, 512:1024]:
                h.tensor_tensor(o, a, b, OP.add), ['RS'], [ktok], tag=f"krope{g}{sc}")
            dve(lambda h, o=XNS[:, 0:512], a=kap: h.tensor_tensor(o, a, a, OP.mult),
                [ktok], [('XNS', 0)])
            prow = pb[0:1, 0:512]
            mm(prow, ONESC[:, :], XNS[:, 0:512], ['ONESC', ('XNS', 0)], btok,
               start=True, stop=True, tag="knorm")
            nt = ('NM', 'k', g)
            if sc == 0:
                dve(lambda h, o=NORMS[0:1, 32 + g:33 + g], i_=prow, b=btok: h.reduce_max(o, i_, AX),
                    [btok], [nt])
            else:
                dve(lambda h, o=NORMS[0:1, 36:37], i_=prow, b=btok: h.reduce_max(o, i_, AX),
                    [btok], [('NM', 'tmp')])
                dve(lambda h, o=NORMS[0:1, 32 + g:33 + g], a=NORMS[0:1, 32 + g:33 + g],
                    b=NORMS[0:1, 36:37]: h.tensor_tensor(o, a, b, OP.max),
                    [nt, ('NM', 'tmp')], [nt])

    for t in range(16):
        emit_ln_tile(t)
        if t % 4 == 3:
            emit_k_sc(t // 4)
    for h_ in range(H):
        emit_qbuild_head(0, h_)
    emit_qnorm(H - 1)

    dma('pool', 'wsm0', WSM[:, 0:4096].rearrange("p (c f) -> p c f", c=16),
        dwv.ap().rearrange("(c p) f -> p c f", p=128), [], [('WSM', 0), ('WSM', 1)], tag="ldwv")
    for t in range(16):
        pb, btok = pphalf(1, t % 2)
        pv = pb[:, 0:256]
        for dc in range(16):
            mm(pv, xnt_ap(dc, t * 128, (t + 1) * 128), WSM[:, dc * 256:(dc + 1) * 256],
               [('WSM', 0), ('WSM', 1)] + xnt_toks([dc], t * 128, (t + 1) * 128),
               btok, start=(dc == 0), stop=(dc == 15), tag=f"v{t}")
        act(lambda h, o=VSB[:, t * 256:(t + 1) * 256], i_=pv: h.copy(o, i_),
            [btok], [('VSB', t)], tag=f"vev{t}")

    pending_out_prev = []
    emit_out_band_prev = None
    for q in range(NQT):
        qs0 = q * QL

        if q > 0:
            dma('sync', 'ropec', XNSF[:, 0:512], drope[:, qs0:qs0 + 512],
                [], [('XNS', 0), ('XNS', 1)], tag=f"rc{q}")
            dma('sync', 'ropes', XNSF[:, 512:1024], drope[:, 2048 + qs0: 2048 + qs0 + 512],
                [], [('XNS', 2), ('XNS', 3)], tag=f"rs{q}")
            for h_ in range(H):
                emit_qbuild_head(q, h_)

        if pending_out_prev:
            for band in pending_out_prev:
                emit_out_band_prev(band)
            pending_out_prev = []

        if q == 0:
            nr = [('NM', 'q', hh) for hh in range(H)] + [('NM', 'k', 0), ('NM', 'k', 1)]
            dve(lambda h, o=NORMS[0:1, 40:48], i_=NORMS[0:1, 0:8], s=NORMS[0:1, 32:33]:
                h.tensor_single_scalar(o, i_, s, OP.add), nr, [('NM', 'c')])
            dve(lambda h, o=NORMS[0:1, 48:56], i_=NORMS[0:1, 8:16], s=NORMS[0:1, 33:34]:
                h.tensor_single_scalar(o, i_, s, OP.add), nr + [('NM', 'c')], [('NM', 'c')])
            dve(lambda h, o=NORMS[0:1, 40:56], i_=NORMS[0:1, 40:56]:
                h.tensor_single_scalar(o, i_, -1.6 / 256.0, OP.mult), [('NM', 'c')], [('NM', 'c')])
            pbias = PP[2][:, 512:528]
            mm(pbias, ONESR[:, :], NORMS[0:1, 40:56], ['ONESR', ('NM', 'c')], ('PB', 2, 1),
               start=True, stop=True, tag="cbias")
            act(lambda h, o=SMALL[:, 18:34], i_=pbias: h.copy(o, i_), [('PB', 2, 1)],
                [('SM', 'biasC')])

        wo_dma_recs = []
        for dmq in range(4):
            for half in range(2):
                dst = SCR[:, half * 4096:(half + 1) * 4096].rearrange("p (c f) -> p c f", c=8)
                src = dwo[half * 1024:(half + 1) * 1024, dmq * 512:(dmq + 1) * 512].rearrange(
                    "(c p) f -> p c f", p=128)
                wo_dma_recs.append((dst, src, reg_toks('SCR', half * 4096, (half + 1) * 4096),
                                    f"ldwo{q}_{dmq}_{half}"))

        def emit_finalize(hh):
            sp = hh % 2
            prow_f = PP[3][0:1, sp * 512: sp * 512 + 512]
            pbc = PP[3][:, sp * 512:(sp + 1) * 512]
            pv_f = PP[2][:, sp * 512: sp * 512 + 512]
            dve(lambda h, o=RROW[:, :], i_=prow_f: h.reciprocal(o, i_),
                [('PB', 3, sp)], ['RROW'])
            mm(pbc, ONESR[:, :], RROW[:, :], ['ONESR', 'RROW'], ('PB', 3, sp),
               start=True, stop=True, tag=f"bc{q}_{hh}")
            rtile = ACC[:, 1024:1536]
            act(lambda h, o=rtile, i_=pbc: h.copy(o, i_), [('PB', 3, sp)], acc_toks(1024, 1536))
            dve(lambda h, o=VAC[:, hh * 512:(hh + 1) * 512], a=pv_f, b=rtile:
                h.tensor_tensor(o, a, b, OP.mult), [('PB', 2, sp)] + acc_toks(1024, 1536),
                [('VAC', hh)], tag=f"nv{q}_{hh}")

        for h_ in range(H):
            g = h_ // 8
            vtok = ('PB', 2, h_ % 2)
            pv = PP[2][:, (h_ % 2) * 512:(h_ % 2) * 512 + 512]
            prow = PP[3][0:1, (h_ % 2) * 512:(h_ % 2) * 512 + 512]

            def stage_scores(jp):
                pi = jp % 2
                for u in range(2):
                    j = 2 * jp + u
                    pb, btok = pphalf(pi, u)
                    mm(pb, KT[:, g * 2048 + j * 128: g * 2048 + (j + 1) * 128],
                       QT[:, h_ * 512:(h_ + 1) * 512],
                       [('KT', g * 4 + j // 4), ('QT', h_)], btok,
                       start=True, stop=True, tag=f"s{q}_{h_}_{j}")
                esl = jp % 2
                eap = XNS[:, esl * 1024:(esl + 1) * 1024]
                etoks = [('XNS', 2 * esl), ('XNS', 2 * esl + 1)]
                act(lambda h, o=eap, i_=PP[pi][:, :], b=SMALL[:, 18 + h_:19 + h_]:
                    h.activation(o, i_, AF.Exp, bias=b, scale=1.0 / 128.0),
                    [('PB', pi, 0), ('PB', pi, 1), ('SM', 'biasC')], etoks, tag=f"e{q}_{h_}_{jp}")

            def stage_consume(jp):
                esl = jp % 2
                eap = XNS[:, esl * 1024:(esl + 1) * 1024]
                etoks = [('XNS', 2 * esl), ('XNS', 2 * esl + 1)]
                for u in range(2):
                    j = 2 * jp + u
                    mm(pv, VSB[:, j * 256 + g * 128: j * 256 + (g + 1) * 128],
                       eap[:, u * 512:(u + 1) * 512],
                       [('VSB', j)] + etoks, vtok,
                       start=(jp == 0 and u == 0), stop=(jp == 7 and u == 1), tag=f"va{q}_{h_}_{j}")
                for u in range(2):
                    mm(prow, ONESC[:, :], eap[:, u * 512:(u + 1) * 512],
                       ['ONESC'] + etoks, ('PB', 3, h_ % 2),
                       start=(jp == 0 and u == 0), stop=(jp == 7 and u == 1), tag=f"sm{q}_{h_}_{jp}")

            for jp in range(9):
                if jp < 8:
                    stage_scores(jp)
                if jp == 2 and h_ > 0:
                    emit_finalize(h_ - 1)
                if jp >= 1:
                    stage_consume(jp - 1)
        emit_finalize(H - 1)

        for dmq in range(4):
            for half in range(2):
                dst_, src_, wtoks, tg = wo_dma_recs[dmq * 2 + half]
                dma('pool', f"wo{half}", dst_, src_, [], wtoks, tag=tg)
            for half in range(2):
                wtoks = reg_toks('SCR', half * 4096, (half + 1) * 4096)
                for qsub in range(4):
                    pb, btok = pphalf(qsub // 2, qsub % 2)
                    for h_ in range(8 * half, 8 * half + 8):
                        rhs = SCR[:, half * 4096 + (h_ % 8) * 512:
                                  half * 4096 + (h_ % 8 + 1) * 512]
                        mm(pb, VAC[:, h_ * 512 + qsub * 128: h_ * 512 + qsub * 128 + 128], rhs,
                           [('VAC', h_)] + wtoks, btok,
                           start=(h_ == 0), stop=(h_ == 15), tag=f"o{q}_{dmq}_{qsub}")
            for qsub in range(4):
                pb, btok = pphalf(qsub // 2, qsub % 2)
                accap = ACC[:, qsub * 2048 + dmq * 512: qsub * 2048 + (dmq + 1) * 512]
                atoks = acc_toks(qsub * 2048 + dmq * 512, qsub * 2048 + (dmq + 1) * 512)
                act(lambda h, o=accap, i_=pb: h.copy(o, i_), [btok], atoks,
                    tag=f"oev{q}_{dmq}_{qsub}")
            if q == 0:
                prow = PP[3][0:1, 0:512]
                for h_ in range(16):
                    half = h_ // 8
                    wtoks = reg_toks('SCR', half * 4096, (half + 1) * 4096)
                    rhs = SCR[:, half * 4096 + (h_ % 8) * 512: half * 4096 + (h_ % 8 + 1) * 512]
                    mm(prow, BVBF[:, h_:h_ + 1], rhs, ['BVBF'] + wtoks, ('PB', 3, 0),
                       start=(h_ == 0), stop=(h_ == 15), tag=f"r{dmq}")
                dve(lambda h, o=BOR[0:1, dmq * 512:(dmq + 1) * 512],
                    a=BOR[0:1, dmq * 512:(dmq + 1) * 512], b=prow:
                    h.tensor_tensor(o, a, b, OP.add), [('PB', 3, 0), 'BOR'], ['BOR'],
                    tag=f"bor{dmq}")

        def emit_sparsify(b_, stage):
            tl = [4 * b_ + j for j in range(4)]
            htoks = [('SCR', x) for x in tl]
            sB = lambda k2: RSBF[:, k2 * 512:(k2 + 1) * 512]
            aB = lambda j2: XNS[:, j2 * 512:(j2 + 1) * 512]
            xt = [('XNS', j) for j in range(4)]
            if stage == 0:
                for j in range(4):
                    dve(lambda h, o=XNS.ap().bitcast(U16)[:, j * 512:(j + 1) * 512],
                        a=SCR.ap().bitcast(U16)[:, tl[j] * 512:(tl[j] + 1) * 512]:
                        h.tensor_single_scalar(o, a, 0x7FFF, OP.bitwise_and),
                        [htoks[j]], [('XNS', j)], tag=f"abs{b_}_{j}")
            elif stage == 1:
                dve(lambda h, o=sB(0), a=aB(0), b=aB(1): h.tensor_tensor(o, a, b, OP.max),
                    xt, ['RS'])
                dve(lambda h, o=sB(1), a=aB(0), b=aB(1): h.tensor_tensor(o, a, b, OP.min),
                    xt + ['RS'], ['RS'])
                dve(lambda h, o=sB(2), a=aB(2), b=aB(3): h.tensor_tensor(o, a, b, OP.max),
                    xt + ['RS'], ['RS'])
                dve(lambda h, o=sB(3), a=aB(2), b=aB(3): h.tensor_tensor(o, a, b, OP.min),
                    xt + ['RS'], ['RS'])
            elif stage == 2:
                dve(lambda h, o=sB(0), a=sB(0), b=sB(2): h.tensor_tensor(o, a, b, OP.min),
                    ['RS'], ['RS'])
                dve(lambda h, o=sB(1), a=sB(1), b=sB(3): h.tensor_tensor(o, a, b, OP.max),
                    ['RS'], ['RS'])
                dve(lambda h, o=sB(0), a=sB(0), b=sB(1): h.tensor_tensor(o, a, b, OP.max),
                    ['RS'], ['RS'])
            else:
                for j in range(4):
                    dve(lambda h, o=aB(j), a=aB(j), b=sB(0): h.tensor_tensor(o, a, b, OP.is_ge),
                        ['RS', ('XNS', j)], [('XNS', j)], tag=f"ge{j}")
                for j in range(4):
                    dve(lambda h, o=SCR[:, tl[j] * 512:(tl[j] + 1) * 512],
                        a=SCR[:, tl[j] * 512:(tl[j] + 1) * 512], b=aB(j):
                        h.tensor_tensor(o, a, b, OP.mult), [('XNS', j), htoks[j]], [htoks[j]],
                        tag=f"msk{j}")

        for iq in range(4):
            for it in range(16):
                T = iq * 16 + it
                gsl = it % 4
                gtoks = [('QT', 4 * gsl + k) for k in range(4)]
                utoks = [('VAC', 4 * gsl + k) for k in range(4)]
                dma('pool', f"qtg{gsl}", QT[:, gsl * 2048:(gsl + 1) * 2048].rearrange("p (c f) -> p c f", c=16),
                    dwg[T], [], gtoks, tag=f"ldg{q}_{T}")
                dma('pool', f"vacu{gsl}", VAC[:, gsl * 2048:(gsl + 1) * 2048].rearrange("p (c f) -> p c f", c=16),
                    dwu[T], [], utoks, tag=f"ldu{q}_{T}")
                pg, gt = pphalf(0, it % 2)
                pu, ut = pphalf(1, it % 2)
                for dc in range(16):
                    mm(pg, QT[:, gsl * 2048 + dc * 128: gsl * 2048 + (dc + 1) * 128],
                       xnt_ap(dc, qs0, qs0 + 512),
                       gtoks + xnt_toks([dc], qs0, qs0 + 512), gt,
                       start=(dc == 0), stop=(dc == 15), tag=f"g{q}_{T}")
                for dc in range(16):
                    mm(pu, VAC[:, gsl * 2048 + dc * 128: gsl * 2048 + (dc + 1) * 128],
                       xnt_ap(dc, qs0, qs0 + 512),
                       utoks + xnt_toks([dc], qs0, qs0 + 512), ut,
                       start=(dc == 0), stop=(dc == 15), tag=f"u{q}_{T}")
                htok = ('SCR', it)
                act(lambda h, o=SCR[:, it * 512:(it + 1) * 512], i_=pg: h.activation(o, i_, AF.Silu),
                    [gt], [htok], tag=f"si{q}_{T}")
                dve(lambda h, o=SCR[:, it * 512:(it + 1) * 512],
                    a=SCR[:, it * 512:(it + 1) * 512], b=pu:
                    h.tensor_tensor(o, a, b, OP.mult), [htok, ut], [htok], tag=f"h{q}_{T}")
                for b_ in range(4):
                    s_ = it - (4 * b_ + 3)
                    if 0 <= s_ <= 3:
                        emit_sparsify(b_, s_)
            for b_ in range(4):
                for s_ in range(4):
                    if 4 * b_ + 3 + s_ > 15:
                        emit_sparsify(b_, s_)
            for b_ in []:
                tl = [4 * j + b_ for j in range(4)]
                htoks = [('SCR', x) for x in tl]
                for j in range(4):
                    dve(lambda h, o=XNS.ap().bitcast(U16)[:, j * 512:(j + 1) * 512],
                        a=SCR.ap().bitcast(U16)[:, tl[j] * 512:(tl[j] + 1) * 512]:
                        h.tensor_single_scalar(o, a, 0x7FFF, OP.bitwise_and),
                        [htoks[j]], [('XNS', j)], tag=f"abs{b_}_{j}")
                sB = lambda k: RSBF[:, k * 512:(k + 1) * 512]
                aB = lambda j: XNS[:, j * 512:(j + 1) * 512]
                xt = [('XNS', j) for j in range(4)]
                dve(lambda h, o=sB(0), a=aB(0), b=aB(1): h.tensor_tensor(o, a, b, OP.max),
                    xt, ['RS'])
                dve(lambda h, o=sB(1), a=aB(0), b=aB(1): h.tensor_tensor(o, a, b, OP.min),
                    xt + ['RS'], ['RS'])
                dve(lambda h, o=sB(2), a=aB(2), b=aB(3): h.tensor_tensor(o, a, b, OP.max),
                    xt + ['RS'], ['RS'])
                dve(lambda h, o=sB(3), a=aB(2), b=aB(3): h.tensor_tensor(o, a, b, OP.min),
                    xt + ['RS'], ['RS'])
                dve(lambda h, o=sB(0), a=sB(0), b=sB(2): h.tensor_tensor(o, a, b, OP.min),
                    ['RS'], ['RS'])
                dve(lambda h, o=sB(1), a=sB(1), b=sB(3): h.tensor_tensor(o, a, b, OP.max),
                    ['RS'], ['RS'])
                dve(lambda h, o=sB(0), a=sB(0), b=sB(1): h.tensor_tensor(o, a, b, OP.max),
                    ['RS'], ['RS'])
                for j in range(4):
                    dve(lambda h, o=aB(j), a=aB(j), b=sB(0): h.tensor_tensor(o, a, b, OP.is_ge),
                        ['RS', ('XNS', j)], [('XNS', j)], tag=f"ge{j}")
                for j in range(4):
                    dve(lambda h, o=SCR[:, tl[j] * 512:(tl[j] + 1) * 512],
                        a=SCR[:, tl[j] * 512:(tl[j] + 1) * 512], b=aB(j):
                        h.tensor_tensor(o, a, b, OP.mult), [('XNS', j), htoks[j]], [htoks[j]],
                        tag=f"msk{j}")
            for ph in range(2):
                for it in range(16):
                    T = iq * 16 + it
                    wsl = (it // 2) % 3
                    wtok = ('WSM', wsl)
                    if it % 2 == 0:
                        dma('pool', f"wsm{wsl}",
                            WSM[:, wsl * 2048:(wsl + 1) * 2048].rearrange("p (t f) -> p t f", t=2),
                            dwd[T * 128:(T + 2) * 128,
                                ph * 1024:(ph + 1) * 1024].rearrange("(t p) f -> p t f", p=128),
                            [], [wtok], tag=f"ldd{q}_{iq}_{ph}_{it}")
                    off = (it % 2) * 1024
                    for band in range(4):
                        for c in range(2):
                            pb, btok = pphalf(band, c)
                            mm(pb, SCR[:, it * 512 + band * 128: it * 512 + band * 128 + 128],
                               WSM[:, wsl * 2048 + off + c * 512: wsl * 2048 + off + (c + 1) * 512],
                               [('SCR', it), wtok], btok,
                               start=(it == 0), stop=(it == 15),
                               tag=f"d{q}_{iq}_{ph}_{it}_{band}{c}")
                for band in range(4):
                    for c in range(2):
                        pb, btok = pphalf(band, c)
                        dmc = 2 * ph + c
                        accap = ACC[:, band * 2048 + dmc * 512: band * 2048 + (dmc + 1) * 512]
                        atoks = acc_toks(band * 2048 + dmc * 512, band * 2048 + (dmc + 1) * 512)
                        dve(lambda h, o=accap, a=accap, b=pb: h.tensor_tensor(o, a, b, OP.add),
                            [btok] + atoks, atoks, tag=f"df{q}_{iq}_{ph}_{band}{c}")

        for c4 in range(4):
            pbc = PP[3][:, 512:1024]
            mm(pbc, ONESR[:, :], BOR[0:1, c4 * 512:(c4 + 1) * 512], ['ONESR', 'BOR'],
               ('PB', 3, 1), start=True, stop=True, tag=f"bob{q}_{c4}")
            act(lambda h, o=VACF[:, c4 * 512:(c4 + 1) * 512], i_=pbc: h.copy(o, i_),
                [('PB', 3, 1)], reg_toks('VAC', c4 * 1024, (c4 + 1) * 1024), tag=f"bobe{q}_{c4}")
        bobtoks = reg_toks('VAC', 0, 4096)

        def emit_out_band(band, qs0=qs0, q=q):
            sl = band % 2
            xap = SCRF[:, sl * 2048:(sl + 1) * 2048]
            xtok = reg_toks('SCR', sl * 4096, (sl + 1) * 4096)
            r0 = qs0 + band * 128
            dma('sync', f"scrh{sl}", xap, dx[r0:r0 + 128, :], [], xtok, tag=f"ox{q}_{band}")
            atoks = acc_toks(band * 2048, (band + 1) * 2048)
            p.op('pool', lambda h, o=xap, a=xap, b=ACC[:, band * 2048:(band + 1) * 2048]:
                 h.tensor_tensor(o, a, b, OP.add), reads=xtok + atoks, writes=xtok,
                 tag=f"oa{q}_{band}")
            p.op('pool', lambda h, o=xap, a=xap, b=VACF[:, 0:2048]:
                 h.tensor_tensor(o, a, b, OP.add), reads=xtok + bobtoks, writes=xtok,
                 tag=f"ob{q}_{band}")
            dma('sync', f"scrh{sl}", dy[r0:r0 + 128, :], xap, xtok, [('Y', q, band)], tag=f"oy{q}_{band}")

        pending_out_prev = [0, 1, 2, 3]
        emit_out_band_prev = emit_out_band

    if pending_out_prev:
        for band in pending_out_prev:
            emit_out_band_prev(band)

    p.finalize()
    for c in list(ENGINES) + p.lanes:
        sems[c] = es.enter_context(nc.semaphore("sem_" + c.replace('dl_', '')))

    with nc.Block() as block:
        @block.tensor
        def _(t):
            p.emit('pe', t, sems)

        @block.scalar
        def _(s_):
            p.emit('act', s_, sems)

        @block.vector
        def _(v):
            p.emit('dve', v, sems)

        @block.gpsimd
        def _(g_):
            p.emit('pool', g_, sems)

        @block.sync
        def _(sy):
            p.emit('sync', sy, sems)
            for c in p.lanes:
                sy.wait_ge(sems[c], 16 * p.cnt[c])

    return nc


_NC = None


def _host_prep(wq, wk, wv, wo, bq, bk, bv, bo, wg, wu, wd):
    f = np.float32
    wq_y = np.ascontiguousarray(wq.reshape(16, 128, 16, 128).transpose(2, 1, 0, 3))
    iqv, bv_, jv, pv = np.meshgrid(np.arange(4), np.arange(4), np.arange(4),
                                   np.arange(128), indexing='ij')
    perm = np.empty(I, dtype=np.int64)
    perm[(iqv * 2048 + bv_ * 512 + jv * 128 + pv).ravel()] = \
        (4 * (512 * iqv + 128 * bv_ + pv) + jv).ravel()
    wg_y = np.ascontiguousarray(wg[:, perm].reshape(16, 128, 64, 128).transpose(2, 1, 0, 3))
    wu_y = np.ascontiguousarray(wu[:, perm].reshape(16, 128, 64, 128).transpose(2, 1, 0, 3))
    wd_p = np.ascontiguousarray(wd[perm, :])
    bq_y = np.ascontiguousarray(bq.reshape(16, 128).T)
    bk_y = np.ascontiguousarray(bk.reshape(2, 128).T)
    bv2 = bv.reshape(2, 128)
    bv_y = np.ascontiguousarray(np.stack([bv2[h // 8] for h in range(16)], axis=1))
    bo_r = np.ascontiguousarray(bo.reshape(1, D))
    inv = 1.0 / 10000 ** (np.arange(0, HD, 2, dtype=np.float64) / HD)
    fr = np.concatenate([inv, inv])
    ang = fr[:, None] * np.arange(S, dtype=np.float64)[None, :]
    cos_t = np.cos(ang)
    sinx = np.sin(ang)
    sinx[0:64, :] = -sinx[0:64, :]
    sinx = np.roll(sinx, -64, axis=0)
    rope = np.concatenate([cos_t, sinx], axis=1).astype(f)
    return dict(wq_y=wq_y, wk=np.ascontiguousarray(wk), wv=np.ascontiguousarray(wv),
                wo=np.ascontiguousarray(wo), wg_y=wg_y, wu_y=wu_y, wd_p=wd_p,
                bq_y=bq_y, bk_y=bk_y, bv_y=bv_y, bo_r=bo_r, rope=rope,
                ident=np.eye(128, dtype=f), ones_c=np.ones((128, 1), dtype=f),
                ones_r=np.ones((1, 128), dtype=f), eps_c=np.full((128, 1), EPS, dtype=f))


def kernel(L0_attn_x, L0_attn_wq, L0_attn_bq, L0_attn_wk, L0_attn_bk,
           L0_attn_wv, L0_attn_bv, L0_attn_wo, L0_attn_bo,
           L0_ffn_w_gate, L0_ffn_w_up, L0_ffn_w_down):
    global _NC
    a32 = lambda v: np.asarray(v, dtype=np.float32)
    x = a32(L0_attn_x)
    common = _host_prep(a32(L0_attn_wq), a32(L0_attn_wk), a32(L0_attn_wv), a32(L0_attn_wo),
                        a32(L0_attn_bq), a32(L0_attn_bk), a32(L0_attn_bv), a32(L0_attn_bo),
                        a32(L0_ffn_w_gate), a32(L0_ffn_w_up), a32(L0_ffn_w_down))
    if _NC is None:
        _NC = build_program()
    in_maps = [dict(common, x=np.ascontiguousarray(x[b])) for b in range(8)]
    res = run_bass_kernel_spmd(_NC, in_maps, core_ids=list(range(8)))
    return np.stack([np.asarray(r["y"], dtype=np.float32) for r in res.results], axis=0)


if __name__ == "__main__":
    nc = build_program()
    print("built ok; sbuf_remaining:", nc.sbuf_bytes_remaining)


# version 22
# speedup vs baseline: 1.0036x; 1.0036x over previous
import sys
if '/opt/trn_rl_repo' not in sys.path:
    sys.path.insert(0, '/opt/trn_rl_repo')

import numpy as np
import ml_dtypes
import concourse.bass as bass
import concourse.mybir as mybir
from concourse.bass_utils import run_bass_kernel_spmd
from contextlib import ExitStack

F32 = mybir.dt.float32
BF16 = mybir.dt.float16
U16 = mybir.dt.uint16
AX = mybir.AxisListType.X
OP = mybir.AluOpType
AF = mybir.ActivationFunctionType

S = 2048
D = 2048
H = 16
G = 2
HD = 128
I = 8192
NQT = 4
QL = 512
EPS = 1e-5

ENGINES = ("pe", "act", "dve", "pool", "sync")


class Prog:

    def __init__(self):
        self.ops = {e: [] for e in ENGINES}
        self.cnt = {c: 0 for c in ENGINES}
        self.tok = {}
        self.marked = {c: set() for c in ENGINES}
        self.sat = {e: {} for e in ENGINES}
        self.lanes = []

    def _dep(self, eng, waits, chan, idx):
        if chan == 'pe' and eng == 'pe':
            return
        if idx <= self.sat[eng].get(chan, -1):
            return
        if waits.get(chan, -1) < idx:
            waits[chan] = idx

    def op(self, eng, fn, reads=(), writes=(), dma=False, lane=None, tag=""):
        if dma:
            chan = 'dl_' + lane
            if chan not in self.cnt:
                self.cnt[chan] = 0
                self.lanes.append(chan)
        else:
            chan = eng
        waits = {}
        for t in reads:
            st = self.tok.get(t)
            if st and st['w'] is not None:
                self._dep(eng, waits, *st['w'])
        for t in writes:
            st = self.tok.get(t)
            if st:
                if st['w'] is not None:
                    self._dep(eng, waits, *st['w'])
                for (c, i) in st['r']:
                    self._dep(eng, waits, c, i)
        idx = self.cnt[chan]
        self.cnt[chan] += 1
        self.ops[eng].append(dict(fn=fn, waits=waits, chan=chan, idx=idx, dma=dma, tag=tag))
        for c, i in waits.items():
            self.sat[eng][c] = max(self.sat[eng].get(c, -1), i)
            if not c.startswith('dl_'):
                self.marked[c].add(i)
        me = (chan, idx)
        for t in reads:
            self.tok.setdefault(t, {'w': None, 'r': []})['r'].append(me)
        for t in writes:
            self.tok[t] = {'w': me, 'r': []}

    def finalize(self):
        self.prefix = {}
        for c in ENGINES:
            pos = {}
            for n, i in enumerate(sorted(self.marked[c])):
                pos[i] = n + 1
            self.prefix[c] = pos

    def val(self, chan, idx):
        if chan.startswith('dl_'):
            return 16 * (idx + 1)
        return self.prefix[chan][idx]

    def emit(self, eng, h, sems):
        for rec in self.ops[eng]:
            for chan in sorted(rec['waits']):
                h.wait_ge(sems[chan], self.val(chan, rec['waits'][chan]))
            ins = rec['fn'](h)
            if rec['dma']:
                ins.then_inc(sems[rec['chan']], 16)
            elif rec['idx'] in self.marked[rec['chan']]:
                ins.then_inc(sems[rec['chan']], 1)


def build_program():
    nc = bass.Bass("TRN2", target_bir_lowering=False)

    dx = nc.dram_tensor("x", [S, D], F32, kind="ExternalInput")
    dwq = nc.dram_tensor("wq_y", [H, 128, 16, 128], F32, kind="ExternalInput")
    dwk = nc.dram_tensor("wk", [D, G * HD], F32, kind="ExternalInput")
    dwv = nc.dram_tensor("wv", [D, G * HD], F32, kind="ExternalInput")
    dwo = nc.dram_tensor("wo", [D, D], F32, kind="ExternalInput")
    dwg = nc.dram_tensor("wg_y", [64, 128, 16, 128], F32, kind="ExternalInput")
    dwu = nc.dram_tensor("wu_y", [64, 128, 16, 128], F32, kind="ExternalInput")
    dwd = nc.dram_tensor("wd_p", [I, D], F32, kind="ExternalInput")
    dbq = nc.dram_tensor("bq_y", [128, H], F32, kind="ExternalInput")
    dbk = nc.dram_tensor("bk_y", [128, G], F32, kind="ExternalInput")
    dbv = nc.dram_tensor("bv_y", [128, H], F32, kind="ExternalInput")
    dbo = nc.dram_tensor("bo_r", [1, D], F32, kind="ExternalInput")
    drope = nc.dram_tensor("rope", [128, 2 * S], F32, kind="ExternalInput")
    dident = nc.dram_tensor("ident", [128, 128], F32, kind="ExternalInput")
    donesc = nc.dram_tensor("ones_c", [128, 1], F32, kind="ExternalInput")
    donesr = nc.dram_tensor("ones_r", [1, 128], F32, kind="ExternalInput")
    deps = nc.dram_tensor("eps_c", [128, 1], F32, kind="ExternalInput")
    dy = nc.dram_tensor("y", [S, D], F32, kind="ExternalOutput")

    es = ExitStack()
    sb = lambda name, shape, dt: es.enter_context(nc.sbuf_tensor(name, shape, dt))
    ps = lambda name, shape, dt: es.enter_context(nc.psum_tensor(name, shape, dt))

    XNT = sb("XNT", [128, 16 * 2048], BF16)
    KT = sb("KT", [128, G * 2048], BF16)
    VSB = sb("VSB", [128, 16 * 256], BF16)
    ACC = sb("ACC", [128, 8192], F32)
    QT = sb("QT", [128, 16 * 512], BF16)
    VAC = sb("VAC", [128, 16 * 512], BF16)
    SCR = sb("SCR", [128, 16 * 512], BF16)
    WSM = sb("WSM", [128, 3 * 2048], BF16)
    XNS = sb("XNS", [128, 2048], BF16)
    RS = sb("RS", [128, 1024], F32)
    SMALL = sb("SMALL", [128, 72], F32)
    BVBF = sb("BVBF", [128, 16], BF16)
    IDENT = sb("IDENT", [128, 128], F32)
    ONESC = sb("ONESC", [128, 1], BF16)
    ONESR = sb("ONESR", [1, 128], F32)
    NORMS = sb("NORMS", [1, 64], F32)
    RROW = sb("RROW", [1, 512], F32)
    BOR = sb("BOR", [1, 2048], F32)

    PP = [ps(f"PP{i}", [128, 1024], F32) for i in range(4)]

    def pphalf(i, half):
        return PP[i][:, half * 512:(half + 1) * 512], ('PB', i, half)

    p = Prog()
    sems = {}

    def mm(dst, lhsT, rhs, reads, wtok, start, stop, tag=""):
        p.op('pe', lambda h, a=dst, b=lhsT, c=rhs, s1=start, s2=stop:
             h.matmul(a, b, c, start=s1, stop=s2), reads=reads, writes=[wtok], tag=tag)

    def transpose(dst, src, reads, wtok, tag=""):
        p.op('pe', lambda h, a=dst, b=src: h.transpose(a, b, IDENT[:, :]),
             reads=list(reads) + ['IDENT'], writes=[wtok], tag=tag)

    def act(fn, reads, writes, tag=""):
        p.op('act', fn, reads=reads, writes=writes, tag=tag)

    def dve(fn, reads, writes, tag=""):
        p.op('dve', fn, reads=reads, writes=writes, tag=tag)

    def copy_any(eng, dst, src, reads, writes, tag=""):
        if eng == 'act':
            p.op('act', lambda h, o=dst, i_=src: h.copy(o, i_), reads=reads, writes=writes, tag=tag)
        else:
            p.op('dve', lambda h, o=dst, i_=src: h.tensor_copy(o, i_), reads=reads, writes=writes, tag=tag)

    def dma(eng, lane, dst, src, reads, writes, tag=""):
        p.op(eng, lambda h, a=dst, b=src: h.dma_start(a, b),
             reads=reads, writes=writes, dma=True, lane=lane, tag=tag)

    def xnt_toks(dc_list, s0, s1):
        t0, t1 = s0 // 128, (s1 + 127) // 128
        return [('XNT', dc, t) for dc in dc_list for t in range(t0, t1)]

    def xnt_ap(dc, s0, s1):
        return XNT[:, dc * 2048 + s0: dc * 2048 + s1]

    def acc_toks(c0, c1):
        return [('ACC', i) for i in range(c0 // 512, (c1 + 511) // 512)]

    def reg_toks(name, c0, c1):
        return [(name, i) for i in range(c0 // 512, (c1 + 511) // 512)]

    QTF = QT.ap().bitcast(F32)
    XNSF = XNS.ap().bitcast(F32)
    VACF = VAC.ap().bitcast(F32)
    SCRF = SCR.ap().bitcast(F32)
    RSBF = RS.ap().bitcast(BF16)

    dma('sync', 'cbq', SMALL[:, 0:16], dbq[:, :], [], [('SM', 'bq')])
    dma('sync', 'cbk', SMALL[:, 16:18], dbk[:, :], [], [('SM', 'bk')])
    dma('sync', 'ceps', SMALL[:, 34:35], deps[:, :], [], [('SM', 'eps')])
    dma('sync', 'cid', IDENT[:, :], dident[:, :], [], ['IDENT'])
    dma('sync', 'conr', ONESR[:, :], donesr[:, :], [], ['ONESR'])
    dma('sync', 'cbor', BOR[:, :], dbo[:, :], [], ['BOR'])
    dma('pool', 'conc', ONESC[:, :], donesc[:, :], [], ['ONESC'])
    dma('pool', 'cbv', BVBF[:, :], dbv[:, :], [], ['BVBF'])
    dma('pool', 'wsm0', WSM[:, 0:4096].rearrange("p (c f) -> p c f", c=16),
        dwk.ap().rearrange("(c p) f -> p c f", p=128), [], [('WSM', 0), ('WSM', 1)], tag="ldwk")
    dma('sync', 'crope', ACC[:, 0:4096], drope[:, :], [], acc_toks(0, 4096))

    def emit_qnorm(hh):
        qsl = 2 + (hh % 2)
        prow = PP[2][0:1, 0:512]
        mm(prow, ONESC[:, :], XNS[:, qsl * 512:(qsl + 1) * 512],
           ['ONESC', ('XNS', qsl)], ('PB', 2, 0), start=True, stop=True, tag=f"qnorm{hh}")
        dve(lambda h, o=NORMS[0:1, hh:hh + 1], i_=prow: h.reduce_max(o, i_, AX),
            [('PB', 2, 0)], [('NM', 'q', hh)])

    def emit_qbuild_head(q, h_):
        qs0 = q * QL
        if q == 0:
            cosl, sinl = ACC[:, 0:512], ACC[:, 2048:2560]
            ctoks = acc_toks(0, 512) + acc_toks(2048, 2560)
        else:
            cosl, sinl = XNSF[:, 0:512], XNSF[:, 512:1024]
            ctoks = [('XNS', i) for i in range(4)]
        sl = (h_ + 2) % 3
        wtok = ('WSM', sl)
        dma('pool', f"wsm{sl}", WSM[:, sl * 2048:(sl + 1) * 2048].rearrange("p (c f) -> p c f", c=16),
            dwq[h_], [], [wtok], tag=f"ldwq{q}_{h_}")
        pb, btok = pphalf(0, h_ % 2)
        for dc in range(16):
            mm(pb, WSM[:, sl * 2048 + dc * 128: sl * 2048 + (dc + 1) * 128],
               xnt_ap(dc, qs0, qs0 + 512),
               [wtok] + xnt_toks([dc], qs0, qs0 + 512),
               btok, start=(dc == 0), stop=(dc == 15), tag=f"q{q}_{h_}")
        act(lambda h, o=RS[:, 0:512], i_=pb, b=SMALL[:, h_:h_ + 1]:
            h.activation(o, i_, AF.Identity, bias=b), [btok, ('SM', 'bq')], ['RS'])
        dve(lambda h, o=RS[0:64, 512:1024], a=RS[64:128, 0:512], b=sinl[64:128, :]:
            h.tensor_tensor(o, a, b, OP.mult), ['RS'] + ctoks, ['RS'])
        dve(lambda h, o=RS[64:128, 512:1024], a=RS[0:64, 0:512], b=sinl[0:64, :]:
            h.tensor_tensor(o, a, b, OP.mult), ['RS'] + ctoks, ['RS'])
        dve(lambda h, o=RS[:, 0:512], a=RS[:, 0:512], b=cosl:
            h.tensor_tensor(o, a, b, OP.mult), ['RS'] + ctoks, ['RS'])
        qap = QT[:, h_ * 512:(h_ + 1) * 512]
        qtok = ('QT', h_)
        dve(lambda h, o=qap, a=RS[:, 0:512], b=RS[:, 512:1024]:
            h.tensor_tensor(o, a, b, OP.add), ['RS'], [qtok], tag=f"qrope{q}_{h_}")
        if q == 0:
            qsl = 2 + (h_ % 2)
            dve(lambda h, o=XNS[:, qsl * 512:(qsl + 1) * 512], a=qap:
                h.tensor_tensor(o, a, a, OP.mult), [qtok], [('XNS', qsl)])
            if h_ > 0:
                emit_qnorm(h_ - 1)

    KTF = KT.ap().bitcast(F32)
    VSBF = VSB.ap().bitcast(F32)

    def emit_ln_tile(t):
        sl = t % 4
        if sl == 0:
            xap, xtok, xlane = SCRF[:, 0:2048], reg_toks('SCR', 0, 4096), 'scrh0'
        elif sl == 1:
            xap, xtok, xlane = SCRF[:, 2048:4096], reg_toks('SCR', 4096, 8192), 'scrh1'
        elif sl == 2:
            xap, xtok, xlane = ACC[:, 4096:6144], acc_toks(4096, 6144), 'lnkt'
        else:
            xap, xtok, xlane = ACC[:, 6144:8192], acc_toks(6144, 8192), 'lnvs'
        if sl < 2:
            cap = QTF[:, sl * 2048:(sl + 1) * 2048]
            ctok = reg_toks('QT', sl * 4096, (sl + 1) * 4096)
        else:
            cap = VACF[:, (sl - 2) * 2048:(sl - 1) * 2048]
            ctok = reg_toks('VAC', (sl - 2) * 4096, (sl - 1) * 4096)
        dma('sync', xlane, xap, dx[t * 128:(t + 1) * 128, :], [], xtok, tag=f"ldx{t}")
        c0 = 36 + sl * 9
        s1 = SMALL[:, c0:c0 + 1]
        st1 = ('SM', 's1', sl)
        s2 = SMALL[:, c0 + 1:c0 + 2]
        t2 = ('SM', 's2', sl)
        dve(lambda h, o=s1, i_=xap: h.reduce_sum(o, i_, AX), xtok, [st1])
        dve(lambda h, o=RS[:, 0:1024], i0=xap[:, 0:1024], i1=xap[:, 0:1024], a=s2:
            h.scalar_tensor_tensor(o, i0, 1.0, i1, OP.mult, OP.mult, accum_out=a),
            xtok, ['RS', t2], tag=f"sqa{t}")
        s2b = SMALL[:, c0 + 2:c0 + 3]
        t2b = ('SM', 's2b', sl)
        dve(lambda h, o=RS[:, 0:1024], i0=xap[:, 1024:2048], i1=xap[:, 1024:2048], a=s2b:
            h.scalar_tensor_tensor(o, i0, 1.0, i1, OP.mult, OP.mult, accum_out=a),
            xtok, ['RS', t2b], tag=f"sqb{t}")
        m = SMALL[:, c0 + 3:c0 + 4]
        mt = ('SM', 'm', sl)
        dve(lambda h, o=m, i_=s1: h.tensor_single_scalar(o, i_, 1.0 / D, OP.mult), [st1], [mt])
        v1 = SMALL[:, c0 + 4:c0 + 5]
        vt1 = ('SM', 'v1', sl)
        dve(lambda h, o=v1, a=s2, b=s2b: h.tensor_tensor(o, a, b, OP.add), [t2, t2b], [vt1])
        msq = SMALL[:, c0 + 5:c0 + 6]
        msqt = ('SM', 'msq', sl)
        dve(lambda h, o=msq, i_=m: h.tensor_tensor(o, i_, i_, OP.mult), [mt], [msqt])
        dve(lambda h, o=v1, a=v1, s=1.0 / D, b=msq: h.scalar_tensor_tensor(o, a, s, b, OP.mult, OP.subtract),
            [vt1, msqt], [vt1])
        std = SMALL[:, c0 + 6:c0 + 7]
        stdt = ('SM', 'std', sl)
        act(lambda h, o=std, i_=v1: h.activation(o, i_, AF.Sqrt, bias=SMALL[:, 34:35]),
            [vt1, ('SM', 'eps')], [stdt])
        rstd = SMALL[:, c0 + 7:c0 + 8]
        rt = ('SM', 'rstd', sl)
        dve(lambda h, o=rstd, i_=std: h.reciprocal(o, i_), [stdt], [rt])
        nmr = SMALL[:, c0 + 8:c0 + 9]
        nmrt = ('SM', 'nmr', sl)
        dve(lambda h, o=nmr, i0=m, b=rstd: h.scalar_tensor_tensor(o, i0, -1.0, b, OP.mult, OP.mult),
            [mt, rt], [nmrt])
        act(lambda h, o=cap, i_=xap, sc_=rstd, b=nmr:
            h.activation(o, i_, AF.Identity, bias=b, scale=sc_),
            xtok + [rt, nmrt], ctok, tag=f"ln{t}")
        for grp in range(4):
            pb, btok = pphalf(2 + grp // 2, grp % 2)
            for k in range(4):
                dc = grp * 4 + k
                transpose(pb[:, k * 128:(k + 1) * 128], cap[:, dc * 128:(dc + 1) * 128],
                          ctok, btok, tag=f"tp{t}_{dc}")
            dst = XNT.ap().rearrange("p (c f) -> p c f", f=2048)[:, grp * 4:grp * 4 + 4,
                                                                t * 128:(t + 1) * 128]
            srcv = pb.rearrange("p (c f) -> p c f", f=128)
            copy_any('act', dst, srcv, [btok],
                     [('XNT', grp * 4 + k, t) for k in range(4)], tag=f"ev{t}_{grp}")

    knorm_pending = []

    def emit_knorm_pending():
        while knorm_pending:
            g, sc, ksl = knorm_pending.pop(0)
            prow = PP[2][0:1, 0:512]
            mm(prow, ONESC[:, :], XNS[:, ksl * 512:(ksl + 1) * 512],
               ['ONESC', ('XNS', ksl)], ('PB', 2, 0), start=True, stop=True, tag="knorm")
            nt = ('NM', 'k', g)
            if sc == 0:
                dve(lambda h, o=NORMS[0:1, 32 + g:33 + g], i_=prow: h.reduce_max(o, i_, AX),
                    [('PB', 2, 0)], [nt])
            else:
                dve(lambda h, o=NORMS[0:1, 36:37], i_=prow: h.reduce_max(o, i_, AX),
                    [('PB', 2, 0)], [('NM', 'tmp')])
                dve(lambda h, o=NORMS[0:1, 32 + g:33 + g], a=NORMS[0:1, 32 + g:33 + g],
                    b=NORMS[0:1, 36:37]: h.tensor_tensor(o, a, b, OP.max),
                    [nt, ('NM', 'tmp')], [nt])

    def emit_k_sc(sc):
        emit_knorm_pending()
        for g in range(G):
            pb, btok = pphalf(0, (2 * sc + g) % 2)
            for dc in range(16):
                mm(pb, WSM[:, dc * 256 + g * 128: dc * 256 + (g + 1) * 128],
                   xnt_ap(dc, sc * 512, (sc + 1) * 512),
                   [('WSM', 0), ('WSM', 1)] + xnt_toks([dc], sc * 512, (sc + 1) * 512),
                   btok, start=(dc == 0), stop=(dc == 15), tag=f"k{g}{sc}")
            act(lambda h, o=RS[:, 0:512], i_=pb, b=SMALL[:, 16 + g:17 + g]:
                h.activation(o, i_, AF.Identity, bias=b), [btok, ('SM', 'bk')], ['RS'])
            cosl = ACC[:, sc * 512:(sc + 1) * 512]
            sinl = ACC[:, 2048 + sc * 512: 2048 + (sc + 1) * 512]
            ctoks = acc_toks(sc * 512, (sc + 1) * 512) + acc_toks(2048 + sc * 512,
                                                                  2048 + (sc + 1) * 512)
            dve(lambda h, o=RS[0:64, 512:1024], a=RS[64:128, 0:512], b=sinl[64:128, :]:
                h.tensor_tensor(o, a, b, OP.mult), ['RS'] + ctoks, ['RS'])
            dve(lambda h, o=RS[64:128, 512:1024], a=RS[0:64, 0:512], b=sinl[0:64, :]:
                h.tensor_tensor(o, a, b, OP.mult), ['RS'] + ctoks, ['RS'])
            dve(lambda h, o=RS[:, 0:512], a=RS[:, 0:512], b=cosl:
                h.tensor_tensor(o, a, b, OP.mult), ['RS'] + ctoks, ['RS'])
            kap = KT[:, g * 2048 + sc * 512: g * 2048 + (sc + 1) * 512]
            ktok = ('KT', g * 4 + sc)
            dve(lambda h, o=kap, a=RS[:, 0:512], b=RS[:, 512:1024]:
                h.tensor_tensor(o, a, b, OP.add), ['RS'], [ktok], tag=f"krope{g}{sc}")
            ksl = (2 * sc + g) % 2
            dve(lambda h, o=XNS[:, ksl * 512:(ksl + 1) * 512], a=kap:
                h.tensor_tensor(o, a, a, OP.mult), [ktok], [('XNS', ksl)])
            knorm_pending.append((g, sc, ksl))

    for t in range(16):
        emit_ln_tile(t)
        if t % 4 == 3:
            emit_k_sc(t // 4)
    emit_knorm_pending()
    for h_ in range(H):
        emit_qbuild_head(0, h_)
    emit_qnorm(H - 1)

    dma('pool', 'wsm0', WSM[:, 0:4096].rearrange("p (c f) -> p c f", c=16),
        dwv.ap().rearrange("(c p) f -> p c f", p=128), [], [('WSM', 0), ('WSM', 1)], tag="ldwv")
    for t in range(16):
        pb, btok = pphalf(1, t % 2)
        pv = pb[:, 0:256]
        for dc in range(16):
            mm(pv, xnt_ap(dc, t * 128, (t + 1) * 128), WSM[:, dc * 256:(dc + 1) * 256],
               [('WSM', 0), ('WSM', 1)] + xnt_toks([dc], t * 128, (t + 1) * 128),
               btok, start=(dc == 0), stop=(dc == 15), tag=f"v{t}")
        act(lambda h, o=VSB[:, t * 256:(t + 1) * 256], i_=pv: h.copy(o, i_),
            [btok], [('VSB', t)], tag=f"vev{t}")

    pending_out_prev = []
    emit_out_band_prev = None
    for q in range(NQT):
        qs0 = q * QL

        if q > 0:
            dma('sync', 'ropec', XNSF[:, 0:512], drope[:, qs0:qs0 + 512],
                [], [('XNS', 0), ('XNS', 1)], tag=f"rc{q}")
            dma('sync', 'ropes', XNSF[:, 512:1024], drope[:, 2048 + qs0: 2048 + qs0 + 512],
                [], [('XNS', 2), ('XNS', 3)], tag=f"rs{q}")
            for h_ in range(H):
                emit_qbuild_head(q, h_)

        if pending_out_prev:
            for band in pending_out_prev:
                emit_out_band_prev(band)
            pending_out_prev = []

        if q == 0:
            nr = [('NM', 'q', hh) for hh in range(H)] + [('NM', 'k', 0), ('NM', 'k', 1)]
            dve(lambda h, o=NORMS[0:1, 40:48], i_=NORMS[0:1, 0:8], s=NORMS[0:1, 32:33]:
                h.tensor_single_scalar(o, i_, s, OP.add), nr, [('NM', 'c')])
            dve(lambda h, o=NORMS[0:1, 48:56], i_=NORMS[0:1, 8:16], s=NORMS[0:1, 33:34]:
                h.tensor_single_scalar(o, i_, s, OP.add), nr + [('NM', 'c')], [('NM', 'c')])
            dve(lambda h, o=NORMS[0:1, 40:56], i_=NORMS[0:1, 40:56]:
                h.tensor_single_scalar(o, i_, -1.6 / 256.0, OP.mult), [('NM', 'c')], [('NM', 'c')])
            pbias = PP[2][:, 512:528]
            mm(pbias, ONESR[:, :], NORMS[0:1, 40:56], ['ONESR', ('NM', 'c')], ('PB', 2, 1),
               start=True, stop=True, tag="cbias")
            act(lambda h, o=SMALL[:, 18:34], i_=pbias: h.copy(o, i_), [('PB', 2, 1)],
                [('SM', 'biasC')])

        wo_dma_recs = []
        for dmq in range(4):
            for half in range(2):
                dst = SCR[:, half * 4096:(half + 1) * 4096].rearrange("p (c f) -> p c f", c=8)
                src = dwo[half * 1024:(half + 1) * 1024, dmq * 512:(dmq + 1) * 512].rearrange(
                    "(c p) f -> p c f", p=128)
                wo_dma_recs.append((dst, src, reg_toks('SCR', half * 4096, (half + 1) * 4096),
                                    f"ldwo{q}_{dmq}_{half}"))

        def emit_finalize(hh):
            sp = hh % 2
            prow_f = PP[3][0:1, sp * 512: sp * 512 + 512]
            pbc = PP[3][:, sp * 512:(sp + 1) * 512]
            pv_f = PP[2][:, sp * 512: sp * 512 + 512]
            dve(lambda h, o=RROW[:, :], i_=prow_f: h.reciprocal(o, i_),
                [('PB', 3, sp)], ['RROW'])
            mm(pbc, ONESR[:, :], RROW[:, :], ['ONESR', 'RROW'], ('PB', 3, sp),
               start=True, stop=True, tag=f"bc{q}_{hh}")
            rtile = ACC[:, 1024:1536]
            act(lambda h, o=rtile, i_=pbc: h.copy(o, i_), [('PB', 3, sp)], acc_toks(1024, 1536))
            dve(lambda h, o=VAC[:, hh * 512:(hh + 1) * 512], a=pv_f, b=rtile:
                h.tensor_tensor(o, a, b, OP.mult), [('PB', 2, sp)] + acc_toks(1024, 1536),
                [('VAC', hh)], tag=f"nv{q}_{hh}")

        for h_ in range(H):
            g = h_ // 8
            vtok = ('PB', 2, h_ % 2)
            pv = PP[2][:, (h_ % 2) * 512:(h_ % 2) * 512 + 512]
            prow = PP[3][0:1, (h_ % 2) * 512:(h_ % 2) * 512 + 512]

            def stage_scores(jp):
                pi = jp % 2
                for u in range(2):
                    j = 2 * jp + u
                    pb, btok = pphalf(pi, u)
                    mm(pb, KT[:, g * 2048 + j * 128: g * 2048 + (j + 1) * 128],
                       QT[:, h_ * 512:(h_ + 1) * 512],
                       [('KT', g * 4 + j // 4), ('QT', h_)], btok,
                       start=True, stop=True, tag=f"s{q}_{h_}_{j}")
                esl = jp % 2
                eap = XNS[:, esl * 1024:(esl + 1) * 1024]
                etoks = [('XNS', 2 * esl), ('XNS', 2 * esl + 1)]
                act(lambda h, o=eap, i_=PP[pi][:, :], b=SMALL[:, 18 + h_:19 + h_]:
                    h.activation(o, i_, AF.Exp, bias=b, scale=1.0 / 128.0),
                    [('PB', pi, 0), ('PB', pi, 1), ('SM', 'biasC')], etoks, tag=f"e{q}_{h_}_{jp}")

            def stage_consume(jp):
                esl = jp % 2
                eap = XNS[:, esl * 1024:(esl + 1) * 1024]
                etoks = [('XNS', 2 * esl), ('XNS', 2 * esl + 1)]
                for u in range(2):
                    j = 2 * jp + u
                    mm(pv, VSB[:, j * 256 + g * 128: j * 256 + (g + 1) * 128],
                       eap[:, u * 512:(u + 1) * 512],
                       [('VSB', j)] + etoks, vtok,
                       start=(jp == 0 and u == 0), stop=(jp == 7 and u == 1), tag=f"va{q}_{h_}_{j}")
                for u in range(2):
                    mm(prow, ONESC[:, :], eap[:, u * 512:(u + 1) * 512],
                       ['ONESC'] + etoks, ('PB', 3, h_ % 2),
                       start=(jp == 0 and u == 0), stop=(jp == 7 and u == 1), tag=f"sm{q}_{h_}_{jp}")

            for jp in range(9):
                if jp < 8:
                    stage_scores(jp)
                if jp == 2 and h_ > 0:
                    emit_finalize(h_ - 1)
                if jp >= 1:
                    stage_consume(jp - 1)
        emit_finalize(H - 1)

        for dmq in range(4):
            for half in range(2):
                dst_, src_, wtoks, tg = wo_dma_recs[dmq * 2 + half]
                dma('pool', f"wo{half}", dst_, src_, [], wtoks, tag=tg)
            for half in range(2):
                wtoks = reg_toks('SCR', half * 4096, (half + 1) * 4096)
                for qsub in range(4):
                    pb, btok = pphalf(qsub // 2, qsub % 2)
                    for h_ in range(8 * half, 8 * half + 8):
                        rhs = SCR[:, half * 4096 + (h_ % 8) * 512:
                                  half * 4096 + (h_ % 8 + 1) * 512]
                        mm(pb, VAC[:, h_ * 512 + qsub * 128: h_ * 512 + qsub * 128 + 128], rhs,
                           [('VAC', h_)] + wtoks, btok,
                           start=(h_ == 0), stop=(h_ == 15), tag=f"o{q}_{dmq}_{qsub}")
            for qsub in range(4):
                pb, btok = pphalf(qsub // 2, qsub % 2)
                accap = ACC[:, qsub * 2048 + dmq * 512: qsub * 2048 + (dmq + 1) * 512]
                atoks = acc_toks(qsub * 2048 + dmq * 512, qsub * 2048 + (dmq + 1) * 512)
                act(lambda h, o=accap, i_=pb: h.copy(o, i_), [btok], atoks,
                    tag=f"oev{q}_{dmq}_{qsub}")
            if q == 0:
                prow = PP[3][0:1, 0:512]
                for h_ in range(16):
                    half = h_ // 8
                    wtoks = reg_toks('SCR', half * 4096, (half + 1) * 4096)
                    rhs = SCR[:, half * 4096 + (h_ % 8) * 512: half * 4096 + (h_ % 8 + 1) * 512]
                    mm(prow, BVBF[:, h_:h_ + 1], rhs, ['BVBF'] + wtoks, ('PB', 3, 0),
                       start=(h_ == 0), stop=(h_ == 15), tag=f"r{dmq}")
                dve(lambda h, o=BOR[0:1, dmq * 512:(dmq + 1) * 512],
                    a=BOR[0:1, dmq * 512:(dmq + 1) * 512], b=prow:
                    h.tensor_tensor(o, a, b, OP.add), [('PB', 3, 0), 'BOR'], ['BOR'],
                    tag=f"bor{dmq}")

        def emit_sparsify(b_, stage):
            tl = [4 * b_ + j for j in range(4)]
            htoks = [('SCR', x) for x in tl]
            sB = lambda k2: RSBF[:, k2 * 512:(k2 + 1) * 512]
            aB = lambda j2: XNS[:, j2 * 512:(j2 + 1) * 512]
            xt = [('XNS', j) for j in range(4)]
            if stage == 0:
                for j in range(4):
                    dve(lambda h, o=XNS.ap().bitcast(U16)[:, j * 512:(j + 1) * 512],
                        a=SCR.ap().bitcast(U16)[:, tl[j] * 512:(tl[j] + 1) * 512]:
                        h.tensor_single_scalar(o, a, 0x7FFF, OP.bitwise_and),
                        [htoks[j]], [('XNS', j)], tag=f"abs{b_}_{j}")
            elif stage == 1:
                dve(lambda h, o=sB(0), a=aB(0), b=aB(1): h.tensor_tensor(o, a, b, OP.max),
                    xt, ['RS'])
                dve(lambda h, o=sB(1), a=aB(0), b=aB(1): h.tensor_tensor(o, a, b, OP.min),
                    xt + ['RS'], ['RS'])
                dve(lambda h, o=sB(2), a=aB(2), b=aB(3): h.tensor_tensor(o, a, b, OP.max),
                    xt + ['RS'], ['RS'])
                dve(lambda h, o=sB(3), a=aB(2), b=aB(3): h.tensor_tensor(o, a, b, OP.min),
                    xt + ['RS'], ['RS'])
            elif stage == 2:
                dve(lambda h, o=sB(0), a=sB(0), b=sB(2): h.tensor_tensor(o, a, b, OP.min),
                    ['RS'], ['RS'])
                dve(lambda h, o=sB(1), a=sB(1), b=sB(3): h.tensor_tensor(o, a, b, OP.max),
                    ['RS'], ['RS'])
                dve(lambda h, o=sB(0), a=sB(0), b=sB(1): h.tensor_tensor(o, a, b, OP.max),
                    ['RS'], ['RS'])
            else:
                for j in range(4):
                    dve(lambda h, o=aB(j), a=aB(j), b=sB(0): h.tensor_tensor(o, a, b, OP.is_ge),
                        ['RS', ('XNS', j)], [('XNS', j)], tag=f"ge{j}")
                for j in range(4):
                    dve(lambda h, o=SCR[:, tl[j] * 512:(tl[j] + 1) * 512],
                        a=SCR[:, tl[j] * 512:(tl[j] + 1) * 512], b=aB(j):
                        h.tensor_tensor(o, a, b, OP.mult), [('XNS', j), htoks[j]], [htoks[j]],
                        tag=f"msk{j}")

        for iq in range(4):
            for it in range(16):
                T = iq * 16 + it
                gsl = it % 4
                gtoks = [('QT', 4 * gsl + k) for k in range(4)]
                utoks = [('VAC', 4 * gsl + k) for k in range(4)]
                dma('pool', f"qtg{gsl}", QT[:, gsl * 2048:(gsl + 1) * 2048].rearrange("p (c f) -> p c f", c=16),
                    dwg[T], [], gtoks, tag=f"ldg{q}_{T}")
                dma('pool', f"vacu{gsl}", VAC[:, gsl * 2048:(gsl + 1) * 2048].rearrange("p (c f) -> p c f", c=16),
                    dwu[T], [], utoks, tag=f"ldu{q}_{T}")
                pg, gt = pphalf(0, it % 2)
                pu, ut = pphalf(1, it % 2)
                for dc in range(16):
                    mm(pg, QT[:, gsl * 2048 + dc * 128: gsl * 2048 + (dc + 1) * 128],
                       xnt_ap(dc, qs0, qs0 + 512),
                       gtoks + xnt_toks([dc], qs0, qs0 + 512), gt,
                       start=(dc == 0), stop=(dc == 15), tag=f"g{q}_{T}")
                for dc in range(16):
                    mm(pu, VAC[:, gsl * 2048 + dc * 128: gsl * 2048 + (dc + 1) * 128],
                       xnt_ap(dc, qs0, qs0 + 512),
                       utoks + xnt_toks([dc], qs0, qs0 + 512), ut,
                       start=(dc == 0), stop=(dc == 15), tag=f"u{q}_{T}")
                htok = ('SCR', it)
                act(lambda h, o=SCR[:, it * 512:(it + 1) * 512], i_=pg: h.activation(o, i_, AF.Silu),
                    [gt], [htok], tag=f"si{q}_{T}")
                dve(lambda h, o=SCR[:, it * 512:(it + 1) * 512],
                    a=SCR[:, it * 512:(it + 1) * 512], b=pu:
                    h.tensor_tensor(o, a, b, OP.mult), [htok, ut], [htok], tag=f"h{q}_{T}")
                for b_ in range(4):
                    s_ = it - (4 * b_ + 3)
                    if 0 <= s_ <= 3:
                        emit_sparsify(b_, s_)
            for b_ in range(4):
                for s_ in range(4):
                    if 4 * b_ + 3 + s_ > 15:
                        emit_sparsify(b_, s_)
            for b_ in []:
                tl = [4 * j + b_ for j in range(4)]
                htoks = [('SCR', x) for x in tl]
                for j in range(4):
                    dve(lambda h, o=XNS.ap().bitcast(U16)[:, j * 512:(j + 1) * 512],
                        a=SCR.ap().bitcast(U16)[:, tl[j] * 512:(tl[j] + 1) * 512]:
                        h.tensor_single_scalar(o, a, 0x7FFF, OP.bitwise_and),
                        [htoks[j]], [('XNS', j)], tag=f"abs{b_}_{j}")
                sB = lambda k: RSBF[:, k * 512:(k + 1) * 512]
                aB = lambda j: XNS[:, j * 512:(j + 1) * 512]
                xt = [('XNS', j) for j in range(4)]
                dve(lambda h, o=sB(0), a=aB(0), b=aB(1): h.tensor_tensor(o, a, b, OP.max),
                    xt, ['RS'])
                dve(lambda h, o=sB(1), a=aB(0), b=aB(1): h.tensor_tensor(o, a, b, OP.min),
                    xt + ['RS'], ['RS'])
                dve(lambda h, o=sB(2), a=aB(2), b=aB(3): h.tensor_tensor(o, a, b, OP.max),
                    xt + ['RS'], ['RS'])
                dve(lambda h, o=sB(3), a=aB(2), b=aB(3): h.tensor_tensor(o, a, b, OP.min),
                    xt + ['RS'], ['RS'])
                dve(lambda h, o=sB(0), a=sB(0), b=sB(2): h.tensor_tensor(o, a, b, OP.min),
                    ['RS'], ['RS'])
                dve(lambda h, o=sB(1), a=sB(1), b=sB(3): h.tensor_tensor(o, a, b, OP.max),
                    ['RS'], ['RS'])
                dve(lambda h, o=sB(0), a=sB(0), b=sB(1): h.tensor_tensor(o, a, b, OP.max),
                    ['RS'], ['RS'])
                for j in range(4):
                    dve(lambda h, o=aB(j), a=aB(j), b=sB(0): h.tensor_tensor(o, a, b, OP.is_ge),
                        ['RS', ('XNS', j)], [('XNS', j)], tag=f"ge{j}")
                for j in range(4):
                    dve(lambda h, o=SCR[:, tl[j] * 512:(tl[j] + 1) * 512],
                        a=SCR[:, tl[j] * 512:(tl[j] + 1) * 512], b=aB(j):
                        h.tensor_tensor(o, a, b, OP.mult), [('XNS', j), htoks[j]], [htoks[j]],
                        tag=f"msk{j}")
            for ph in range(2):
                for it in range(16):
                    T = iq * 16 + it
                    wsl = (it // 2) % 3
                    wtok = ('WSM', wsl)
                    if it % 2 == 0:
                        dma('pool', f"wsm{wsl}",
                            WSM[:, wsl * 2048:(wsl + 1) * 2048].rearrange("p (t f) -> p t f", t=2),
                            dwd[T * 128:(T + 2) * 128,
                                ph * 1024:(ph + 1) * 1024].rearrange("(t p) f -> p t f", p=128),
                            [], [wtok], tag=f"ldd{q}_{iq}_{ph}_{it}")
                    off = (it % 2) * 1024
                    for band in range(4):
                        for c in range(2):
                            pb, btok = pphalf(band, c)
                            mm(pb, SCR[:, it * 512 + band * 128: it * 512 + band * 128 + 128],
                               WSM[:, wsl * 2048 + off + c * 512: wsl * 2048 + off + (c + 1) * 512],
                               [('SCR', it), wtok], btok,
                               start=(it == 0), stop=(it == 15),
                               tag=f"d{q}_{iq}_{ph}_{it}_{band}{c}")
                for band in range(4):
                    for c in range(2):
                        pb, btok = pphalf(band, c)
                        dmc = 2 * ph + c
                        accap = ACC[:, band * 2048 + dmc * 512: band * 2048 + (dmc + 1) * 512]
                        atoks = acc_toks(band * 2048 + dmc * 512, band * 2048 + (dmc + 1) * 512)
                        dve(lambda h, o=accap, a=accap, b=pb: h.tensor_tensor(o, a, b, OP.add),
                            [btok] + atoks, atoks, tag=f"df{q}_{iq}_{ph}_{band}{c}")

        for c4 in range(4):
            pbc = PP[3][:, 512:1024]
            mm(pbc, ONESR[:, :], BOR[0:1, c4 * 512:(c4 + 1) * 512], ['ONESR', 'BOR'],
               ('PB', 3, 1), start=True, stop=True, tag=f"bob{q}_{c4}")
            act(lambda h, o=VACF[:, c4 * 512:(c4 + 1) * 512], i_=pbc: h.copy(o, i_),
                [('PB', 3, 1)], reg_toks('VAC', c4 * 1024, (c4 + 1) * 1024), tag=f"bobe{q}_{c4}")
        bobtoks = reg_toks('VAC', 0, 4096)

        def emit_out_band(band, qs0=qs0, q=q):
            sl = band % 2
            xap = SCRF[:, sl * 2048:(sl + 1) * 2048]
            xtok = reg_toks('SCR', sl * 4096, (sl + 1) * 4096)
            r0 = qs0 + band * 128
            dma('sync', f"scrh{sl}", xap, dx[r0:r0 + 128, :], [], xtok, tag=f"ox{q}_{band}")
            atoks = acc_toks(band * 2048, (band + 1) * 2048)
            p.op('pool', lambda h, o=xap, a=xap, b=ACC[:, band * 2048:(band + 1) * 2048]:
                 h.tensor_tensor(o, a, b, OP.add), reads=xtok + atoks, writes=xtok,
                 tag=f"oa{q}_{band}")
            p.op('pool', lambda h, o=xap, a=xap, b=VACF[:, 0:2048]:
                 h.tensor_tensor(o, a, b, OP.add), reads=xtok + bobtoks, writes=xtok,
                 tag=f"ob{q}_{band}")
            dma('sync', f"scrh{sl}", dy[r0:r0 + 128, :], xap, xtok, [('Y', q, band)], tag=f"oy{q}_{band}")

        pending_out_prev = [0, 1, 2, 3]
        emit_out_band_prev = emit_out_band

    if pending_out_prev:
        for band in pending_out_prev:
            emit_out_band_prev(band)

    p.finalize()
    for c in list(ENGINES) + p.lanes:
        sems[c] = es.enter_context(nc.semaphore("sem_" + c.replace('dl_', '')))

    with nc.Block() as block:
        @block.tensor
        def _(t):
            p.emit('pe', t, sems)

        @block.scalar
        def _(s_):
            p.emit('act', s_, sems)

        @block.vector
        def _(v):
            p.emit('dve', v, sems)

        @block.gpsimd
        def _(g_):
            p.emit('pool', g_, sems)

        @block.sync
        def _(sy):
            p.emit('sync', sy, sems)
            for c in p.lanes:
                sy.wait_ge(sems[c], 16 * p.cnt[c])

    return nc


_NC = None


def _host_prep(wq, wk, wv, wo, bq, bk, bv, bo, wg, wu, wd):
    f = np.float32
    wq_y = np.ascontiguousarray(wq.reshape(16, 128, 16, 128).transpose(2, 1, 0, 3))
    iqv, bv_, jv, pv = np.meshgrid(np.arange(4), np.arange(4), np.arange(4),
                                   np.arange(128), indexing='ij')
    perm = np.empty(I, dtype=np.int64)
    perm[(iqv * 2048 + bv_ * 512 + jv * 128 + pv).ravel()] = \
        (4 * (512 * iqv + 128 * bv_ + pv) + jv).ravel()
    wg_y = np.ascontiguousarray(wg[:, perm].reshape(16, 128, 64, 128).transpose(2, 1, 0, 3))
    wu_y = np.ascontiguousarray(wu[:, perm].reshape(16, 128, 64, 128).transpose(2, 1, 0, 3))
    wd_p = np.ascontiguousarray(wd[perm, :])
    bq_y = np.ascontiguousarray(bq.reshape(16, 128).T)
    bk_y = np.ascontiguousarray(bk.reshape(2, 128).T)
    bv2 = bv.reshape(2, 128)
    bv_y = np.ascontiguousarray(np.stack([bv2[h // 8] for h in range(16)], axis=1))
    bo_r = np.ascontiguousarray(bo.reshape(1, D))
    inv = 1.0 / 10000 ** (np.arange(0, HD, 2, dtype=np.float64) / HD)
    fr = np.concatenate([inv, inv])
    ang = fr[:, None] * np.arange(S, dtype=np.float64)[None, :]
    cos_t = np.cos(ang)
    sinx = np.sin(ang)
    sinx[0:64, :] = -sinx[0:64, :]
    sinx = np.roll(sinx, -64, axis=0)
    rope = np.concatenate([cos_t, sinx], axis=1).astype(f)
    return dict(wq_y=wq_y, wk=np.ascontiguousarray(wk), wv=np.ascontiguousarray(wv),
                wo=np.ascontiguousarray(wo), wg_y=wg_y, wu_y=wu_y, wd_p=wd_p,
                bq_y=bq_y, bk_y=bk_y, bv_y=bv_y, bo_r=bo_r, rope=rope,
                ident=np.eye(128, dtype=f), ones_c=np.ones((128, 1), dtype=f),
                ones_r=np.ones((1, 128), dtype=f), eps_c=np.full((128, 1), EPS, dtype=f))


def kernel(L0_attn_x, L0_attn_wq, L0_attn_bq, L0_attn_wk, L0_attn_bk,
           L0_attn_wv, L0_attn_bv, L0_attn_wo, L0_attn_bo,
           L0_ffn_w_gate, L0_ffn_w_up, L0_ffn_w_down):
    global _NC
    a32 = lambda v: np.asarray(v, dtype=np.float32)
    x = a32(L0_attn_x)
    common = _host_prep(a32(L0_attn_wq), a32(L0_attn_wk), a32(L0_attn_wv), a32(L0_attn_wo),
                        a32(L0_attn_bq), a32(L0_attn_bk), a32(L0_attn_bv), a32(L0_attn_bo),
                        a32(L0_ffn_w_gate), a32(L0_ffn_w_up), a32(L0_ffn_w_down))
    if _NC is None:
        _NC = build_program()
    in_maps = [dict(common, x=np.ascontiguousarray(x[b])) for b in range(8)]
    res = run_bass_kernel_spmd(_NC, in_maps, core_ids=list(range(8)))
    return np.stack([np.asarray(r["y"], dtype=np.float32) for r in res.results], axis=0)


if __name__ == "__main__":
    nc = build_program()
    print("built ok; sbuf_remaining:", nc.sbuf_bytes_remaining)


# version 23
# speedup vs baseline: 1.0100x; 1.0064x over previous
import sys
if '/opt/trn_rl_repo' not in sys.path:
    sys.path.insert(0, '/opt/trn_rl_repo')

import numpy as np
import ml_dtypes
import concourse.bass as bass
import concourse.mybir as mybir
from concourse.bass_utils import run_bass_kernel_spmd
from contextlib import ExitStack

F32 = mybir.dt.float32
BF16 = mybir.dt.float16
U16 = mybir.dt.uint16
AX = mybir.AxisListType.X
OP = mybir.AluOpType
AF = mybir.ActivationFunctionType

S = 2048
D = 2048
H = 16
G = 2
HD = 128
I = 8192
NQT = 4
QL = 512
EPS = 1e-5

ENGINES = ("pe", "act", "dve", "pool", "sync")


class Prog:

    def __init__(self):
        self.ops = {e: [] for e in ENGINES}
        self.cnt = {c: 0 for c in ENGINES}
        self.tok = {}
        self.marked = {c: set() for c in ENGINES}
        self.sat = {e: {} for e in ENGINES}
        self.lanes = []

    def _dep(self, eng, waits, chan, idx):
        if chan == 'pe' and eng == 'pe':
            return
        if idx <= self.sat[eng].get(chan, -1):
            return
        if waits.get(chan, -1) < idx:
            waits[chan] = idx

    def op(self, eng, fn, reads=(), writes=(), dma=False, lane=None, tag=""):
        if dma:
            chan = 'dl_' + lane
            if chan not in self.cnt:
                self.cnt[chan] = 0
                self.lanes.append(chan)
        else:
            chan = eng
        waits = {}
        for t in reads:
            st = self.tok.get(t)
            if st and st['w'] is not None:
                self._dep(eng, waits, *st['w'])
        for t in writes:
            st = self.tok.get(t)
            if st:
                if st['w'] is not None:
                    self._dep(eng, waits, *st['w'])
                for (c, i) in st['r']:
                    self._dep(eng, waits, c, i)
        idx = self.cnt[chan]
        self.cnt[chan] += 1
        self.ops[eng].append(dict(fn=fn, waits=waits, chan=chan, idx=idx, dma=dma, tag=tag))
        for c, i in waits.items():
            self.sat[eng][c] = max(self.sat[eng].get(c, -1), i)
            if not c.startswith('dl_'):
                self.marked[c].add(i)
        me = (chan, idx)
        for t in reads:
            self.tok.setdefault(t, {'w': None, 'r': []})['r'].append(me)
        for t in writes:
            self.tok[t] = {'w': me, 'r': []}

    def finalize(self):
        self.prefix = {}
        for c in ENGINES:
            pos = {}
            for n, i in enumerate(sorted(self.marked[c])):
                pos[i] = n + 1
            self.prefix[c] = pos

    def val(self, chan, idx):
        if chan.startswith('dl_'):
            return 16 * (idx + 1)
        return self.prefix[chan][idx]

    def emit(self, eng, h, sems):
        for rec in self.ops[eng]:
            for chan in sorted(rec['waits']):
                h.wait_ge(sems[chan], self.val(chan, rec['waits'][chan]))
            ins = rec['fn'](h)
            if rec['dma']:
                ins.then_inc(sems[rec['chan']], 16)
            elif rec['idx'] in self.marked[rec['chan']]:
                ins.then_inc(sems[rec['chan']], 1)


def build_program():
    nc = bass.Bass("TRN2", target_bir_lowering=False)

    dx = nc.dram_tensor("x", [S, D], F32, kind="ExternalInput")
    dwq = nc.dram_tensor("wq_y", [H, 128, 16, 128], F32, kind="ExternalInput")
    dwk = nc.dram_tensor("wk", [D, G * HD], F32, kind="ExternalInput")
    dwv = nc.dram_tensor("wv", [D, G * HD], F32, kind="ExternalInput")
    dwo = nc.dram_tensor("wo", [D, D], F32, kind="ExternalInput")
    dwg = nc.dram_tensor("wg_y", [64, 128, 16, 128], F32, kind="ExternalInput")
    dwu = nc.dram_tensor("wu_y", [64, 128, 16, 128], F32, kind="ExternalInput")
    dwd = nc.dram_tensor("wd_p", [I, D], F32, kind="ExternalInput")
    dbq = nc.dram_tensor("bq_y", [128, H], F32, kind="ExternalInput")
    dbk = nc.dram_tensor("bk_y", [128, G], F32, kind="ExternalInput")
    dbv = nc.dram_tensor("bv_y", [128, H], F32, kind="ExternalInput")
    dbo = nc.dram_tensor("bo_r", [1, D], F32, kind="ExternalInput")
    drope = nc.dram_tensor("rope", [128, 2 * S], F32, kind="ExternalInput")
    dident = nc.dram_tensor("ident", [128, 128], F32, kind="ExternalInput")
    donesc = nc.dram_tensor("ones_c", [128, 1], F32, kind="ExternalInput")
    donesr = nc.dram_tensor("ones_r", [1, 128], F32, kind="ExternalInput")
    deps = nc.dram_tensor("eps_c", [128, 1], F32, kind="ExternalInput")
    dy = nc.dram_tensor("y", [S, D], F32, kind="ExternalOutput")

    es = ExitStack()
    sb = lambda name, shape, dt: es.enter_context(nc.sbuf_tensor(name, shape, dt))
    ps = lambda name, shape, dt: es.enter_context(nc.psum_tensor(name, shape, dt))

    XNT = sb("XNT", [128, 16 * 2048], BF16)
    KT = sb("KT", [128, G * 2048], BF16)
    VSB = sb("VSB", [128, 16 * 256], BF16)
    ACC = sb("ACC", [128, 8192], F32)
    QT = sb("QT", [128, 16 * 512], BF16)
    VAC = sb("VAC", [128, 16 * 512], BF16)
    SCR = sb("SCR", [128, 16 * 512], BF16)
    WSM = sb("WSM", [128, 3 * 2048], BF16)
    XNS = sb("XNS", [128, 2048], BF16)
    RS = sb("RS", [128, 1024], F32)
    SMALL = sb("SMALL", [128, 72], F32)
    BVBF = sb("BVBF", [128, 16], BF16)
    IDENT = sb("IDENT", [128, 128], F32)
    ONESC = sb("ONESC", [128, 1], BF16)
    ONESR = sb("ONESR", [1, 128], F32)
    NORMS = sb("NORMS", [1, 64], F32)
    RROW = sb("RROW", [1, 512], F32)
    BOR = sb("BOR", [1, 2048], F32)

    PP = [ps(f"PP{i}", [128, 1024], F32) for i in range(4)]

    def pphalf(i, half):
        return PP[i][:, half * 512:(half + 1) * 512], ('PB', i, half)

    p = Prog()
    sems = {}

    def mm(dst, lhsT, rhs, reads, wtok, start, stop, tag=""):
        p.op('pe', lambda h, a=dst, b=lhsT, c=rhs, s1=start, s2=stop:
             h.matmul(a, b, c, start=s1, stop=s2), reads=reads, writes=[wtok], tag=tag)

    def transpose(dst, src, reads, wtok, tag=""):
        p.op('pe', lambda h, a=dst, b=src: h.transpose(a, b, IDENT[:, :]),
             reads=list(reads) + ['IDENT'], writes=[wtok], tag=tag)

    def act(fn, reads, writes, tag=""):
        p.op('act', fn, reads=reads, writes=writes, tag=tag)

    def dve(fn, reads, writes, tag=""):
        p.op('dve', fn, reads=reads, writes=writes, tag=tag)

    def copy_any(eng, dst, src, reads, writes, tag=""):
        if eng == 'act':
            p.op('act', lambda h, o=dst, i_=src: h.copy(o, i_), reads=reads, writes=writes, tag=tag)
        else:
            p.op('dve', lambda h, o=dst, i_=src: h.tensor_copy(o, i_), reads=reads, writes=writes, tag=tag)

    def dma(eng, lane, dst, src, reads, writes, tag=""):
        p.op(eng, lambda h, a=dst, b=src: h.dma_start(a, b),
             reads=reads, writes=writes, dma=True, lane=lane, tag=tag)

    def xnt_toks(dc_list, s0, s1):
        t0, t1 = s0 // 128, (s1 + 127) // 128
        return [('XNT', dc, t) for dc in dc_list for t in range(t0, t1)]

    def xnt_ap(dc, s0, s1):
        return XNT[:, dc * 2048 + s0: dc * 2048 + s1]

    def acc_toks(c0, c1):
        return [('ACC', i) for i in range(c0 // 512, (c1 + 511) // 512)]

    def reg_toks(name, c0, c1):
        return [(name, i) for i in range(c0 // 512, (c1 + 511) // 512)]

    QTF = QT.ap().bitcast(F32)
    XNSF = XNS.ap().bitcast(F32)
    VACF = VAC.ap().bitcast(F32)
    SCRF = SCR.ap().bitcast(F32)
    RSBF = RS.ap().bitcast(BF16)

    dma('sync', 'cbq', SMALL[:, 0:16], dbq[:, :], [], [('SM', 'bq')])
    dma('sync', 'cbk', SMALL[:, 16:18], dbk[:, :], [], [('SM', 'bk')])
    dma('sync', 'ceps', SMALL[:, 34:35], deps[:, :], [], [('SM', 'eps')])
    dma('sync', 'cid', IDENT[:, :], dident[:, :], [], ['IDENT'])
    dma('sync', 'conr', ONESR[:, :], donesr[:, :], [], ['ONESR'])
    dma('sync', 'cbor', BOR[:, :], dbo[:, :], [], ['BOR'])
    dma('pool', 'conc', ONESC[:, :], donesc[:, :], [], ['ONESC'])
    dma('pool', 'cbv', BVBF[:, :], dbv[:, :], [], ['BVBF'])
    dma('pool', 'wsm0', WSM[:, 0:4096].rearrange("p (c f) -> p c f", c=16),
        dwk.ap().rearrange("(c p) f -> p c f", p=128), [], [('WSM', 0), ('WSM', 1)], tag="ldwk")
    dma('sync', 'crope', ACC[:, 0:4096], drope[:, :], [], acc_toks(0, 4096))

    def emit_qnorm(hh):
        qsl = 2 + (hh % 2)
        prow = PP[2][0:1, 0:512]
        mm(prow, ONESC[:, :], XNS[:, qsl * 512:(qsl + 1) * 512],
           ['ONESC', ('XNS', qsl)], ('PB', 2, 0), start=True, stop=True, tag=f"qnorm{hh}")
        dve(lambda h, o=NORMS[0:1, hh:hh + 1], i_=prow: h.reduce_max(o, i_, AX),
            [('PB', 2, 0)], [('NM', 'q', hh)])

    def emit_qbuild_head(q, h_):
        qs0 = q * QL
        if q == 0:
            cosl, sinl = ACC[:, 0:512], ACC[:, 2048:2560]
            ctoks = acc_toks(0, 512) + acc_toks(2048, 2560)
        else:
            cosl, sinl = XNSF[:, 0:512], XNSF[:, 512:1024]
            ctoks = [('XNS', i) for i in range(4)]
        sl = (h_ + 2) % 3
        wtok = ('WSM', sl)
        dma('pool', f"wsm{sl}", WSM[:, sl * 2048:(sl + 1) * 2048].rearrange("p (c f) -> p c f", c=16),
            dwq[h_], [], [wtok], tag=f"ldwq{q}_{h_}")
        pb, btok = pphalf(0, h_ % 2)
        for dc in range(16):
            mm(pb, WSM[:, sl * 2048 + dc * 128: sl * 2048 + (dc + 1) * 128],
               xnt_ap(dc, qs0, qs0 + 512),
               [wtok] + xnt_toks([dc], qs0, qs0 + 512),
               btok, start=(dc == 0), stop=(dc == 15), tag=f"q{q}_{h_}")
        act(lambda h, o=RS[:, 0:512], i_=pb, b=SMALL[:, h_:h_ + 1]:
            h.activation(o, i_, AF.Identity, bias=b), [btok, ('SM', 'bq')], ['RS'])
        dve(lambda h, o=RS[0:64, 512:1024], a=RS[64:128, 0:512], b=sinl[64:128, :]:
            h.tensor_tensor(o, a, b, OP.mult), ['RS'] + ctoks, ['RS'])
        dve(lambda h, o=RS[64:128, 512:1024], a=RS[0:64, 0:512], b=sinl[0:64, :]:
            h.tensor_tensor(o, a, b, OP.mult), ['RS'] + ctoks, ['RS'])
        dve(lambda h, o=RS[:, 0:512], a=RS[:, 0:512], b=cosl:
            h.tensor_tensor(o, a, b, OP.mult), ['RS'] + ctoks, ['RS'])
        qap = QT[:, h_ * 512:(h_ + 1) * 512]
        qtok = ('QT', h_)
        dve(lambda h, o=qap, a=RS[:, 0:512], b=RS[:, 512:1024]:
            h.tensor_tensor(o, a, b, OP.add), ['RS'], [qtok], tag=f"qrope{q}_{h_}")
        if q == 0:
            qsl = 2 + (h_ % 2)
            dve(lambda h, o=XNS[:, qsl * 512:(qsl + 1) * 512], a=qap:
                h.tensor_tensor(o, a, a, OP.mult), [qtok], [('XNS', qsl)])
            if h_ > 0:
                emit_qnorm(h_ - 1)

    KTF = KT.ap().bitcast(F32)
    VSBF = VSB.ap().bitcast(F32)

    def emit_ln_tile(t):
        sl = t % 4
        if sl == 0:
            xap, xtok, xlane = SCRF[:, 0:2048], reg_toks('SCR', 0, 4096), 'scrh0'
        elif sl == 1:
            xap, xtok, xlane = SCRF[:, 2048:4096], reg_toks('SCR', 4096, 8192), 'scrh1'
        elif sl == 2:
            xap, xtok, xlane = ACC[:, 4096:6144], acc_toks(4096, 6144), 'lnkt'
        else:
            xap, xtok, xlane = ACC[:, 6144:8192], acc_toks(6144, 8192), 'lnvs'
        if sl < 2:
            cap = QTF[:, sl * 2048:(sl + 1) * 2048]
            ctok = reg_toks('QT', sl * 4096, (sl + 1) * 4096)
        else:
            cap = VACF[:, (sl - 2) * 2048:(sl - 1) * 2048]
            ctok = reg_toks('VAC', (sl - 2) * 4096, (sl - 1) * 4096)
        dma('sync', xlane, xap, dx[t * 128:(t + 1) * 128, :], [], xtok, tag=f"ldx{t}")
        c0 = 36 + sl * 9
        s1 = SMALL[:, c0:c0 + 1]
        st1 = ('SM', 's1', sl)
        s2 = SMALL[:, c0 + 1:c0 + 2]
        t2 = ('SM', 's2', sl)
        dve(lambda h, o=s1, i_=xap: h.reduce_sum(o, i_, AX), xtok, [st1])
        dve(lambda h, o=RS[:, 0:1024], i0=xap[:, 0:1024], i1=xap[:, 0:1024], a=s2:
            h.scalar_tensor_tensor(o, i0, 1.0, i1, OP.mult, OP.mult, accum_out=a),
            xtok, ['RS', t2], tag=f"sqa{t}")
        s2b = SMALL[:, c0 + 2:c0 + 3]
        t2b = ('SM', 's2b', sl)
        dve(lambda h, o=RS[:, 0:1024], i0=xap[:, 1024:2048], i1=xap[:, 1024:2048], a=s2b:
            h.scalar_tensor_tensor(o, i0, 1.0, i1, OP.mult, OP.mult, accum_out=a),
            xtok, ['RS', t2b], tag=f"sqb{t}")
        m = SMALL[:, c0 + 3:c0 + 4]
        mt = ('SM', 'm', sl)
        dve(lambda h, o=m, i_=s1: h.tensor_single_scalar(o, i_, 1.0 / D, OP.mult), [st1], [mt])
        v1 = SMALL[:, c0 + 4:c0 + 5]
        vt1 = ('SM', 'v1', sl)
        dve(lambda h, o=v1, a=s2, b=s2b: h.tensor_tensor(o, a, b, OP.add), [t2, t2b], [vt1])
        msq = SMALL[:, c0 + 5:c0 + 6]
        msqt = ('SM', 'msq', sl)
        dve(lambda h, o=msq, i_=m: h.tensor_tensor(o, i_, i_, OP.mult), [mt], [msqt])
        dve(lambda h, o=v1, a=v1, s=1.0 / D, b=msq: h.scalar_tensor_tensor(o, a, s, b, OP.mult, OP.subtract),
            [vt1, msqt], [vt1])
        std = SMALL[:, c0 + 6:c0 + 7]
        stdt = ('SM', 'std', sl)
        act(lambda h, o=std, i_=v1: h.activation(o, i_, AF.Sqrt, bias=SMALL[:, 34:35]),
            [vt1, ('SM', 'eps')], [stdt])
        rstd = SMALL[:, c0 + 7:c0 + 8]
        rt = ('SM', 'rstd', sl)
        dve(lambda h, o=rstd, i_=std: h.reciprocal(o, i_), [stdt], [rt])
        nmr = SMALL[:, c0 + 8:c0 + 9]
        nmrt = ('SM', 'nmr', sl)
        dve(lambda h, o=nmr, i0=m, b=rstd: h.scalar_tensor_tensor(o, i0, -1.0, b, OP.mult, OP.mult),
            [mt, rt], [nmrt])
        act(lambda h, o=cap, i_=xap, sc_=rstd, b=nmr:
            h.activation(o, i_, AF.Identity, bias=b, scale=sc_),
            xtok + [rt, nmrt], ctok, tag=f"ln{t}")
        for grp in range(4):
            pb, btok = pphalf(2 + grp // 2, grp % 2)
            for k in range(4):
                dc = grp * 4 + k
                transpose(pb[:, k * 128:(k + 1) * 128], cap[:, dc * 128:(dc + 1) * 128],
                          ctok, btok, tag=f"tp{t}_{dc}")
            dst = XNT.ap().rearrange("p (c f) -> p c f", f=2048)[:, grp * 4:grp * 4 + 4,
                                                                t * 128:(t + 1) * 128]
            srcv = pb.rearrange("p (c f) -> p c f", f=128)
            copy_any('act', dst, srcv, [btok],
                     [('XNT', grp * 4 + k, t) for k in range(4)], tag=f"ev{t}_{grp}")

    knorm_pending = []

    def emit_knorm_pending():
        while knorm_pending:
            g, sc, ksl = knorm_pending.pop(0)
            prow = PP[2][0:1, 0:512]
            mm(prow, ONESC[:, :], XNS[:, ksl * 512:(ksl + 1) * 512],
               ['ONESC', ('XNS', ksl)], ('PB', 2, 0), start=True, stop=True, tag="knorm")
            nt = ('NM', 'k', g)
            if sc == 0:
                dve(lambda h, o=NORMS[0:1, 32 + g:33 + g], i_=prow: h.reduce_max(o, i_, AX),
                    [('PB', 2, 0)], [nt])
            else:
                dve(lambda h, o=NORMS[0:1, 36:37], i_=prow: h.reduce_max(o, i_, AX),
                    [('PB', 2, 0)], [('NM', 'tmp')])
                dve(lambda h, o=NORMS[0:1, 32 + g:33 + g], a=NORMS[0:1, 32 + g:33 + g],
                    b=NORMS[0:1, 36:37]: h.tensor_tensor(o, a, b, OP.max),
                    [nt, ('NM', 'tmp')], [nt])

    def emit_k_sc(sc):
        emit_knorm_pending()
        for g in range(G):
            pb, btok = pphalf(0, (2 * sc + g) % 2)
            for dc in range(16):
                mm(pb, WSM[:, dc * 256 + g * 128: dc * 256 + (g + 1) * 128],
                   xnt_ap(dc, sc * 512, (sc + 1) * 512),
                   [('WSM', 0), ('WSM', 1)] + xnt_toks([dc], sc * 512, (sc + 1) * 512),
                   btok, start=(dc == 0), stop=(dc == 15), tag=f"k{g}{sc}")
            act(lambda h, o=RS[:, 0:512], i_=pb, b=SMALL[:, 16 + g:17 + g]:
                h.activation(o, i_, AF.Identity, bias=b), [btok, ('SM', 'bk')], ['RS'])
            cosl = ACC[:, sc * 512:(sc + 1) * 512]
            sinl = ACC[:, 2048 + sc * 512: 2048 + (sc + 1) * 512]
            ctoks = acc_toks(sc * 512, (sc + 1) * 512) + acc_toks(2048 + sc * 512,
                                                                  2048 + (sc + 1) * 512)
            dve(lambda h, o=RS[0:64, 512:1024], a=RS[64:128, 0:512], b=sinl[64:128, :]:
                h.tensor_tensor(o, a, b, OP.mult), ['RS'] + ctoks, ['RS'])
            dve(lambda h, o=RS[64:128, 512:1024], a=RS[0:64, 0:512], b=sinl[0:64, :]:
                h.tensor_tensor(o, a, b, OP.mult), ['RS'] + ctoks, ['RS'])
            dve(lambda h, o=RS[:, 0:512], a=RS[:, 0:512], b=cosl:
                h.tensor_tensor(o, a, b, OP.mult), ['RS'] + ctoks, ['RS'])
            kap = KT[:, g * 2048 + sc * 512: g * 2048 + (sc + 1) * 512]
            ktok = ('KT', g * 4 + sc)
            dve(lambda h, o=kap, a=RS[:, 0:512], b=RS[:, 512:1024]:
                h.tensor_tensor(o, a, b, OP.add), ['RS'], [ktok], tag=f"krope{g}{sc}")
            ksl = (2 * sc + g) % 2
            dve(lambda h, o=XNS[:, ksl * 512:(ksl + 1) * 512], a=kap:
                h.tensor_tensor(o, a, a, OP.mult), [ktok], [('XNS', ksl)])
            knorm_pending.append((g, sc, ksl))

    def warm_mm():
        mm(PP[1][:, 0:64], IDENT[:, :], IDENT[:, 0:64], ['IDENT'], ('PB', 1, 0),
           start=True, stop=True, tag="warm")

    for t in range(16):
        emit_ln_tile(t)
        warm_mm()
        warm_mm()
        if t % 4 == 3:
            emit_k_sc(t // 4)
    emit_knorm_pending()
    for h_ in range(H):
        emit_qbuild_head(0, h_)
    emit_qnorm(H - 1)

    dma('pool', 'wsm0', WSM[:, 0:4096].rearrange("p (c f) -> p c f", c=16),
        dwv.ap().rearrange("(c p) f -> p c f", p=128), [], [('WSM', 0), ('WSM', 1)], tag="ldwv")
    for t in range(16):
        pb, btok = pphalf(1, t % 2)
        pv = pb[:, 0:256]
        for dc in range(16):
            mm(pv, xnt_ap(dc, t * 128, (t + 1) * 128), WSM[:, dc * 256:(dc + 1) * 256],
               [('WSM', 0), ('WSM', 1)] + xnt_toks([dc], t * 128, (t + 1) * 128),
               btok, start=(dc == 0), stop=(dc == 15), tag=f"v{t}")
        act(lambda h, o=VSB[:, t * 256:(t + 1) * 256], i_=pv: h.copy(o, i_),
            [btok], [('VSB', t)], tag=f"vev{t}")

    pending_out_prev = []
    emit_out_band_prev = None
    for q in range(NQT):
        qs0 = q * QL

        if q > 0:
            dma('sync', 'ropec', XNSF[:, 0:512], drope[:, qs0:qs0 + 512],
                [], [('XNS', 0), ('XNS', 1)], tag=f"rc{q}")
            dma('sync', 'ropes', XNSF[:, 512:1024], drope[:, 2048 + qs0: 2048 + qs0 + 512],
                [], [('XNS', 2), ('XNS', 3)], tag=f"rs{q}")
            for h_ in range(H):
                emit_qbuild_head(q, h_)

        if pending_out_prev:
            for band in pending_out_prev:
                emit_out_band_prev(band)
            pending_out_prev = []

        if q == 0:
            nr = [('NM', 'q', hh) for hh in range(H)] + [('NM', 'k', 0), ('NM', 'k', 1)]
            dve(lambda h, o=NORMS[0:1, 40:48], i_=NORMS[0:1, 0:8], s=NORMS[0:1, 32:33]:
                h.tensor_single_scalar(o, i_, s, OP.add), nr, [('NM', 'c')])
            dve(lambda h, o=NORMS[0:1, 48:56], i_=NORMS[0:1, 8:16], s=NORMS[0:1, 33:34]:
                h.tensor_single_scalar(o, i_, s, OP.add), nr + [('NM', 'c')], [('NM', 'c')])
            dve(lambda h, o=NORMS[0:1, 40:56], i_=NORMS[0:1, 40:56]:
                h.tensor_single_scalar(o, i_, -1.6 / 256.0, OP.mult), [('NM', 'c')], [('NM', 'c')])
            pbias = PP[2][:, 512:528]
            mm(pbias, ONESR[:, :], NORMS[0:1, 40:56], ['ONESR', ('NM', 'c')], ('PB', 2, 1),
               start=True, stop=True, tag="cbias")
            act(lambda h, o=SMALL[:, 18:34], i_=pbias: h.copy(o, i_), [('PB', 2, 1)],
                [('SM', 'biasC')])

        wo_dma_recs = []
        for dmq in range(4):
            for half in range(2):
                dst = SCR[:, half * 4096:(half + 1) * 4096].rearrange("p (c f) -> p c f", c=8)
                src = dwo[half * 1024:(half + 1) * 1024, dmq * 512:(dmq + 1) * 512].rearrange(
                    "(c p) f -> p c f", p=128)
                wo_dma_recs.append((dst, src, reg_toks('SCR', half * 4096, (half + 1) * 4096),
                                    f"ldwo{q}_{dmq}_{half}"))

        def emit_finalize(hh):
            sp = hh % 2
            prow_f = PP[3][0:1, sp * 512: sp * 512 + 512]
            pbc = PP[3][:, sp * 512:(sp + 1) * 512]
            pv_f = PP[2][:, sp * 512: sp * 512 + 512]
            dve(lambda h, o=RROW[:, :], i_=prow_f: h.reciprocal(o, i_),
                [('PB', 3, sp)], ['RROW'])
            mm(pbc, ONESR[:, :], RROW[:, :], ['ONESR', 'RROW'], ('PB', 3, sp),
               start=True, stop=True, tag=f"bc{q}_{hh}")
            rtile = ACC[:, 1024:1536]
            act(lambda h, o=rtile, i_=pbc: h.copy(o, i_), [('PB', 3, sp)], acc_toks(1024, 1536))
            dve(lambda h, o=VAC[:, hh * 512:(hh + 1) * 512], a=pv_f, b=rtile:
                h.tensor_tensor(o, a, b, OP.mult), [('PB', 2, sp)] + acc_toks(1024, 1536),
                [('VAC', hh)], tag=f"nv{q}_{hh}")

        for h_ in range(H):
            g = h_ // 8
            vtok = ('PB', 2, h_ % 2)
            pv = PP[2][:, (h_ % 2) * 512:(h_ % 2) * 512 + 512]
            prow = PP[3][0:1, (h_ % 2) * 512:(h_ % 2) * 512 + 512]

            def stage_scores(jp):
                pi = jp % 2
                for u in range(2):
                    j = 2 * jp + u
                    pb, btok = pphalf(pi, u)
                    mm(pb, KT[:, g * 2048 + j * 128: g * 2048 + (j + 1) * 128],
                       QT[:, h_ * 512:(h_ + 1) * 512],
                       [('KT', g * 4 + j // 4), ('QT', h_)], btok,
                       start=True, stop=True, tag=f"s{q}_{h_}_{j}")
                esl = jp % 2
                eap = XNS[:, esl * 1024:(esl + 1) * 1024]
                etoks = [('XNS', 2 * esl), ('XNS', 2 * esl + 1)]
                act(lambda h, o=eap, i_=PP[pi][:, :], b=SMALL[:, 18 + h_:19 + h_]:
                    h.activation(o, i_, AF.Exp, bias=b, scale=1.0 / 128.0),
                    [('PB', pi, 0), ('PB', pi, 1), ('SM', 'biasC')], etoks, tag=f"e{q}_{h_}_{jp}")

            def stage_consume(jp):
                esl = jp % 2
                eap = XNS[:, esl * 1024:(esl + 1) * 1024]
                etoks = [('XNS', 2 * esl), ('XNS', 2 * esl + 1)]
                for u in range(2):
                    j = 2 * jp + u
                    mm(pv, VSB[:, j * 256 + g * 128: j * 256 + (g + 1) * 128],
                       eap[:, u * 512:(u + 1) * 512],
                       [('VSB', j)] + etoks, vtok,
                       start=(jp == 0 and u == 0), stop=(jp == 7 and u == 1), tag=f"va{q}_{h_}_{j}")
                for u in range(2):
                    mm(prow, ONESC[:, :], eap[:, u * 512:(u + 1) * 512],
                       ['ONESC'] + etoks, ('PB', 3, h_ % 2),
                       start=(jp == 0 and u == 0), stop=(jp == 7 and u == 1), tag=f"sm{q}_{h_}_{jp}")

            for jp in range(9):
                if jp < 8:
                    stage_scores(jp)
                if jp == 2 and h_ > 0:
                    emit_finalize(h_ - 1)
                if jp >= 1:
                    stage_consume(jp - 1)
        emit_finalize(H - 1)

        for dmq in range(4):
            for half in range(2):
                dst_, src_, wtoks, tg = wo_dma_recs[dmq * 2 + half]
                dma('pool', f"wo{half}", dst_, src_, [], wtoks, tag=tg)
            for half in range(2):
                wtoks = reg_toks('SCR', half * 4096, (half + 1) * 4096)
                for qsub in range(4):
                    pb, btok = pphalf(qsub // 2, qsub % 2)
                    for h_ in range(8 * half, 8 * half + 8):
                        rhs = SCR[:, half * 4096 + (h_ % 8) * 512:
                                  half * 4096 + (h_ % 8 + 1) * 512]
                        mm(pb, VAC[:, h_ * 512 + qsub * 128: h_ * 512 + qsub * 128 + 128], rhs,
                           [('VAC', h_)] + wtoks, btok,
                           start=(h_ == 0), stop=(h_ == 15), tag=f"o{q}_{dmq}_{qsub}")
            for qsub in range(4):
                pb, btok = pphalf(qsub // 2, qsub % 2)
                accap = ACC[:, qsub * 2048 + dmq * 512: qsub * 2048 + (dmq + 1) * 512]
                atoks = acc_toks(qsub * 2048 + dmq * 512, qsub * 2048 + (dmq + 1) * 512)
                act(lambda h, o=accap, i_=pb: h.copy(o, i_), [btok], atoks,
                    tag=f"oev{q}_{dmq}_{qsub}")
            if q == 0:
                prow = PP[3][0:1, 0:512]
                for h_ in range(16):
                    half = h_ // 8
                    wtoks = reg_toks('SCR', half * 4096, (half + 1) * 4096)
                    rhs = SCR[:, half * 4096 + (h_ % 8) * 512: half * 4096 + (h_ % 8 + 1) * 512]
                    mm(prow, BVBF[:, h_:h_ + 1], rhs, ['BVBF'] + wtoks, ('PB', 3, 0),
                       start=(h_ == 0), stop=(h_ == 15), tag=f"r{dmq}")
                dve(lambda h, o=BOR[0:1, dmq * 512:(dmq + 1) * 512],
                    a=BOR[0:1, dmq * 512:(dmq + 1) * 512], b=prow:
                    h.tensor_tensor(o, a, b, OP.add), [('PB', 3, 0), 'BOR'], ['BOR'],
                    tag=f"bor{dmq}")

        def emit_sparsify(b_, stage):
            tl = [4 * b_ + j for j in range(4)]
            htoks = [('SCR', x) for x in tl]
            sB = lambda k2: RSBF[:, k2 * 512:(k2 + 1) * 512]
            aB = lambda j2: XNS[:, j2 * 512:(j2 + 1) * 512]
            xt = [('XNS', j) for j in range(4)]
            if stage == 0:
                for j in range(4):
                    dve(lambda h, o=XNS.ap().bitcast(U16)[:, j * 512:(j + 1) * 512],
                        a=SCR.ap().bitcast(U16)[:, tl[j] * 512:(tl[j] + 1) * 512]:
                        h.tensor_single_scalar(o, a, 0x7FFF, OP.bitwise_and),
                        [htoks[j]], [('XNS', j)], tag=f"abs{b_}_{j}")
            elif stage == 1:
                dve(lambda h, o=sB(0), a=aB(0), b=aB(1): h.tensor_tensor(o, a, b, OP.max),
                    xt, ['RS'])
                dve(lambda h, o=sB(1), a=aB(0), b=aB(1): h.tensor_tensor(o, a, b, OP.min),
                    xt + ['RS'], ['RS'])
                dve(lambda h, o=sB(2), a=aB(2), b=aB(3): h.tensor_tensor(o, a, b, OP.max),
                    xt + ['RS'], ['RS'])
                dve(lambda h, o=sB(3), a=aB(2), b=aB(3): h.tensor_tensor(o, a, b, OP.min),
                    xt + ['RS'], ['RS'])
            elif stage == 2:
                dve(lambda h, o=sB(0), a=sB(0), b=sB(2): h.tensor_tensor(o, a, b, OP.min),
                    ['RS'], ['RS'])
                dve(lambda h, o=sB(1), a=sB(1), b=sB(3): h.tensor_tensor(o, a, b, OP.max),
                    ['RS'], ['RS'])
                dve(lambda h, o=sB(0), a=sB(0), b=sB(1): h.tensor_tensor(o, a, b, OP.max),
                    ['RS'], ['RS'])
            else:
                for j in range(4):
                    dve(lambda h, o=aB(j), a=aB(j), b=sB(0): h.tensor_tensor(o, a, b, OP.is_ge),
                        ['RS', ('XNS', j)], [('XNS', j)], tag=f"ge{j}")
                for j in range(4):
                    dve(lambda h, o=SCR[:, tl[j] * 512:(tl[j] + 1) * 512],
                        a=SCR[:, tl[j] * 512:(tl[j] + 1) * 512], b=aB(j):
                        h.tensor_tensor(o, a, b, OP.mult), [('XNS', j), htoks[j]], [htoks[j]],
                        tag=f"msk{j}")

        for iq in range(4):
            for it in range(16):
                T = iq * 16 + it
                gsl = it % 4
                gtoks = [('QT', 4 * gsl + k) for k in range(4)]
                utoks = [('VAC', 4 * gsl + k) for k in range(4)]
                dma('pool', f"qtg{gsl}", QT[:, gsl * 2048:(gsl + 1) * 2048].rearrange("p (c f) -> p c f", c=16),
                    dwg[T], [], gtoks, tag=f"ldg{q}_{T}")
                dma('pool', f"vacu{gsl}", VAC[:, gsl * 2048:(gsl + 1) * 2048].rearrange("p (c f) -> p c f", c=16),
                    dwu[T], [], utoks, tag=f"ldu{q}_{T}")
                pg, gt = pphalf(0, it % 2)
                pu, ut = pphalf(1, it % 2)
                for dc in range(16):
                    mm(pg, QT[:, gsl * 2048 + dc * 128: gsl * 2048 + (dc + 1) * 128],
                       xnt_ap(dc, qs0, qs0 + 512),
                       gtoks + xnt_toks([dc], qs0, qs0 + 512), gt,
                       start=(dc == 0), stop=(dc == 15), tag=f"g{q}_{T}")
                for dc in range(16):
                    mm(pu, VAC[:, gsl * 2048 + dc * 128: gsl * 2048 + (dc + 1) * 128],
                       xnt_ap(dc, qs0, qs0 + 512),
                       utoks + xnt_toks([dc], qs0, qs0 + 512), ut,
                       start=(dc == 0), stop=(dc == 15), tag=f"u{q}_{T}")
                htok = ('SCR', it)
                act(lambda h, o=SCR[:, it * 512:(it + 1) * 512], i_=pg: h.activation(o, i_, AF.Silu),
                    [gt], [htok], tag=f"si{q}_{T}")
                dve(lambda h, o=SCR[:, it * 512:(it + 1) * 512],
                    a=SCR[:, it * 512:(it + 1) * 512], b=pu:
                    h.tensor_tensor(o, a, b, OP.mult), [htok, ut], [htok], tag=f"h{q}_{T}")
                for b_ in range(4):
                    s_ = it - (4 * b_ + 3)
                    if 0 <= s_ <= 3:
                        emit_sparsify(b_, s_)
            for b_ in range(4):
                for s_ in range(4):
                    if 4 * b_ + 3 + s_ > 15:
                        emit_sparsify(b_, s_)
            for b_ in []:
                tl = [4 * j + b_ for j in range(4)]
                htoks = [('SCR', x) for x in tl]
                for j in range(4):
                    dve(lambda h, o=XNS.ap().bitcast(U16)[:, j * 512:(j + 1) * 512],
                        a=SCR.ap().bitcast(U16)[:, tl[j] * 512:(tl[j] + 1) * 512]:
                        h.tensor_single_scalar(o, a, 0x7FFF, OP.bitwise_and),
                        [htoks[j]], [('XNS', j)], tag=f"abs{b_}_{j}")
                sB = lambda k: RSBF[:, k * 512:(k + 1) * 512]
                aB = lambda j: XNS[:, j * 512:(j + 1) * 512]
                xt = [('XNS', j) for j in range(4)]
                dve(lambda h, o=sB(0), a=aB(0), b=aB(1): h.tensor_tensor(o, a, b, OP.max),
                    xt, ['RS'])
                dve(lambda h, o=sB(1), a=aB(0), b=aB(1): h.tensor_tensor(o, a, b, OP.min),
                    xt + ['RS'], ['RS'])
                dve(lambda h, o=sB(2), a=aB(2), b=aB(3): h.tensor_tensor(o, a, b, OP.max),
                    xt + ['RS'], ['RS'])
                dve(lambda h, o=sB(3), a=aB(2), b=aB(3): h.tensor_tensor(o, a, b, OP.min),
                    xt + ['RS'], ['RS'])
                dve(lambda h, o=sB(0), a=sB(0), b=sB(2): h.tensor_tensor(o, a, b, OP.min),
                    ['RS'], ['RS'])
                dve(lambda h, o=sB(1), a=sB(1), b=sB(3): h.tensor_tensor(o, a, b, OP.max),
                    ['RS'], ['RS'])
                dve(lambda h, o=sB(0), a=sB(0), b=sB(1): h.tensor_tensor(o, a, b, OP.max),
                    ['RS'], ['RS'])
                for j in range(4):
                    dve(lambda h, o=aB(j), a=aB(j), b=sB(0): h.tensor_tensor(o, a, b, OP.is_ge),
                        ['RS', ('XNS', j)], [('XNS', j)], tag=f"ge{j}")
                for j in range(4):
                    dve(lambda h, o=SCR[:, tl[j] * 512:(tl[j] + 1) * 512],
                        a=SCR[:, tl[j] * 512:(tl[j] + 1) * 512], b=aB(j):
                        h.tensor_tensor(o, a, b, OP.mult), [('XNS', j), htoks[j]], [htoks[j]],
                        tag=f"msk{j}")
            for ph in range(2):
                for it in range(16):
                    T = iq * 16 + it
                    wsl = (it // 2) % 3
                    wtok = ('WSM', wsl)
                    if it % 2 == 0:
                        dma('pool', f"wsm{wsl}",
                            WSM[:, wsl * 2048:(wsl + 1) * 2048].rearrange("p (t f) -> p t f", t=2),
                            dwd[T * 128:(T + 2) * 128,
                                ph * 1024:(ph + 1) * 1024].rearrange("(t p) f -> p t f", p=128),
                            [], [wtok], tag=f"ldd{q}_{iq}_{ph}_{it}")
                    off = (it % 2) * 1024
                    for band in range(4):
                        for c in range(2):
                            pb, btok = pphalf(band, c)
                            mm(pb, SCR[:, it * 512 + band * 128: it * 512 + band * 128 + 128],
                               WSM[:, wsl * 2048 + off + c * 512: wsl * 2048 + off + (c + 1) * 512],
                               [('SCR', it), wtok], btok,
                               start=(it == 0), stop=(it == 15),
                               tag=f"d{q}_{iq}_{ph}_{it}_{band}{c}")
                for band in range(4):
                    for c in range(2):
                        pb, btok = pphalf(band, c)
                        dmc = 2 * ph + c
                        accap = ACC[:, band * 2048 + dmc * 512: band * 2048 + (dmc + 1) * 512]
                        atoks = acc_toks(band * 2048 + dmc * 512, band * 2048 + (dmc + 1) * 512)
                        dve(lambda h, o=accap, a=accap, b=pb: h.tensor_tensor(o, a, b, OP.add),
                            [btok] + atoks, atoks, tag=f"df{q}_{iq}_{ph}_{band}{c}")

        for c4 in range(4):
            pbc = PP[3][:, 512:1024]
            mm(pbc, ONESR[:, :], BOR[0:1, c4 * 512:(c4 + 1) * 512], ['ONESR', 'BOR'],
               ('PB', 3, 1), start=True, stop=True, tag=f"bob{q}_{c4}")
            act(lambda h, o=VACF[:, c4 * 512:(c4 + 1) * 512], i_=pbc: h.copy(o, i_),
                [('PB', 3, 1)], reg_toks('VAC', c4 * 1024, (c4 + 1) * 1024), tag=f"bobe{q}_{c4}")
        bobtoks = reg_toks('VAC', 0, 4096)

        def emit_out_band(band, qs0=qs0, q=q):
            sl = band % 2
            xap = SCRF[:, sl * 2048:(sl + 1) * 2048]
            xtok = reg_toks('SCR', sl * 4096, (sl + 1) * 4096)
            r0 = qs0 + band * 128
            dma('sync', f"scrh{sl}", xap, dx[r0:r0 + 128, :], [], xtok, tag=f"ox{q}_{band}")
            atoks = acc_toks(band * 2048, (band + 1) * 2048)
            p.op('pool', lambda h, o=xap, a=xap, b=ACC[:, band * 2048:(band + 1) * 2048]:
                 h.tensor_tensor(o, a, b, OP.add), reads=xtok + atoks, writes=xtok,
                 tag=f"oa{q}_{band}")
            p.op('pool', lambda h, o=xap, a=xap, b=VACF[:, 0:2048]:
                 h.tensor_tensor(o, a, b, OP.add), reads=xtok + bobtoks, writes=xtok,
                 tag=f"ob{q}_{band}")
            dma('sync', f"scrh{sl}", dy[r0:r0 + 128, :], xap, xtok, [('Y', q, band)], tag=f"oy{q}_{band}")

        pending_out_prev = [0, 1, 2, 3]
        emit_out_band_prev = emit_out_band

    if pending_out_prev:
        for band in pending_out_prev:
            emit_out_band_prev(band)

    p.finalize()
    for c in list(ENGINES) + p.lanes:
        sems[c] = es.enter_context(nc.semaphore("sem_" + c.replace('dl_', '')))

    with nc.Block() as block:
        @block.tensor
        def _(t):
            p.emit('pe', t, sems)

        @block.scalar
        def _(s_):
            p.emit('act', s_, sems)

        @block.vector
        def _(v):
            p.emit('dve', v, sems)

        @block.gpsimd
        def _(g_):
            p.emit('pool', g_, sems)

        @block.sync
        def _(sy):
            p.emit('sync', sy, sems)
            for c in p.lanes:
                sy.wait_ge(sems[c], 16 * p.cnt[c])

    return nc


_NC = None


def _host_prep(wq, wk, wv, wo, bq, bk, bv, bo, wg, wu, wd):
    f = np.float32
    wq_y = np.ascontiguousarray(wq.reshape(16, 128, 16, 128).transpose(2, 1, 0, 3))
    iqv, bv_, jv, pv = np.meshgrid(np.arange(4), np.arange(4), np.arange(4),
                                   np.arange(128), indexing='ij')
    perm = np.empty(I, dtype=np.int64)
    perm[(iqv * 2048 + bv_ * 512 + jv * 128 + pv).ravel()] = \
        (4 * (512 * iqv + 128 * bv_ + pv) + jv).ravel()
    wg_y = np.ascontiguousarray(wg[:, perm].reshape(16, 128, 64, 128).transpose(2, 1, 0, 3))
    wu_y = np.ascontiguousarray(wu[:, perm].reshape(16, 128, 64, 128).transpose(2, 1, 0, 3))
    wd_p = np.ascontiguousarray(wd[perm, :])
    bq_y = np.ascontiguousarray(bq.reshape(16, 128).T)
    bk_y = np.ascontiguousarray(bk.reshape(2, 128).T)
    bv2 = bv.reshape(2, 128)
    bv_y = np.ascontiguousarray(np.stack([bv2[h // 8] for h in range(16)], axis=1))
    bo_r = np.ascontiguousarray(bo.reshape(1, D))
    inv = 1.0 / 10000 ** (np.arange(0, HD, 2, dtype=np.float64) / HD)
    fr = np.concatenate([inv, inv])
    ang = fr[:, None] * np.arange(S, dtype=np.float64)[None, :]
    cos_t = np.cos(ang)
    sinx = np.sin(ang)
    sinx[0:64, :] = -sinx[0:64, :]
    sinx = np.roll(sinx, -64, axis=0)
    rope = np.concatenate([cos_t, sinx], axis=1).astype(f)
    return dict(wq_y=wq_y, wk=np.ascontiguousarray(wk), wv=np.ascontiguousarray(wv),
                wo=np.ascontiguousarray(wo), wg_y=wg_y, wu_y=wu_y, wd_p=wd_p,
                bq_y=bq_y, bk_y=bk_y, bv_y=bv_y, bo_r=bo_r, rope=rope,
                ident=np.eye(128, dtype=f), ones_c=np.ones((128, 1), dtype=f),
                ones_r=np.ones((1, 128), dtype=f), eps_c=np.full((128, 1), EPS, dtype=f))


def kernel(L0_attn_x, L0_attn_wq, L0_attn_bq, L0_attn_wk, L0_attn_bk,
           L0_attn_wv, L0_attn_bv, L0_attn_wo, L0_attn_bo,
           L0_ffn_w_gate, L0_ffn_w_up, L0_ffn_w_down):
    global _NC
    a32 = lambda v: np.asarray(v, dtype=np.float32)
    x = a32(L0_attn_x)
    common = _host_prep(a32(L0_attn_wq), a32(L0_attn_wk), a32(L0_attn_wv), a32(L0_attn_wo),
                        a32(L0_attn_bq), a32(L0_attn_bk), a32(L0_attn_bv), a32(L0_attn_bo),
                        a32(L0_ffn_w_gate), a32(L0_ffn_w_up), a32(L0_ffn_w_down))
    if _NC is None:
        _NC = build_program()
    in_maps = [dict(common, x=np.ascontiguousarray(x[b])) for b in range(8)]
    res = run_bass_kernel_spmd(_NC, in_maps, core_ids=list(range(8)))
    return np.stack([np.asarray(r["y"], dtype=np.float32) for r in res.results], axis=0)


if __name__ == "__main__":
    nc = build_program()
    print("built ok; sbuf_remaining:", nc.sbuf_bytes_remaining)


# version 24
# speedup vs baseline: 1.0125x; 1.0024x over previous
import sys
if '/opt/trn_rl_repo' not in sys.path:
    sys.path.insert(0, '/opt/trn_rl_repo')

import numpy as np
import ml_dtypes
import concourse.bass as bass
import concourse.mybir as mybir
from concourse.bass_utils import run_bass_kernel_spmd
from contextlib import ExitStack

F32 = mybir.dt.float32
BF16 = mybir.dt.float16
U16 = mybir.dt.uint16
AX = mybir.AxisListType.X
OP = mybir.AluOpType
AF = mybir.ActivationFunctionType

S = 2048
D = 2048
H = 16
G = 2
HD = 128
I = 8192
NQT = 4
QL = 512
EPS = 1e-5

ENGINES = ("pe", "act", "dve", "pool", "sync")


class Prog:

    def __init__(self):
        self.ops = {e: [] for e in ENGINES}
        self.cnt = {c: 0 for c in ENGINES}
        self.tok = {}
        self.marked = {c: set() for c in ENGINES}
        self.sat = {e: {} for e in ENGINES}
        self.lanes = []

    def _dep(self, eng, waits, chan, idx):
        if chan == 'pe' and eng == 'pe':
            return
        if idx <= self.sat[eng].get(chan, -1):
            return
        if waits.get(chan, -1) < idx:
            waits[chan] = idx

    def op(self, eng, fn, reads=(), writes=(), dma=False, lane=None, tag=""):
        if dma:
            chan = 'dl_' + lane
            if chan not in self.cnt:
                self.cnt[chan] = 0
                self.lanes.append(chan)
        else:
            chan = eng
        waits = {}
        for t in reads:
            st = self.tok.get(t)
            if st and st['w'] is not None:
                self._dep(eng, waits, *st['w'])
        for t in writes:
            st = self.tok.get(t)
            if st:
                if st['w'] is not None:
                    self._dep(eng, waits, *st['w'])
                for (c, i) in st['r']:
                    self._dep(eng, waits, c, i)
        idx = self.cnt[chan]
        self.cnt[chan] += 1
        self.ops[eng].append(dict(fn=fn, waits=waits, chan=chan, idx=idx, dma=dma, tag=tag))
        for c, i in waits.items():
            self.sat[eng][c] = max(self.sat[eng].get(c, -1), i)
            if not c.startswith('dl_'):
                self.marked[c].add(i)
        me = (chan, idx)
        for t in reads:
            self.tok.setdefault(t, {'w': None, 'r': []})['r'].append(me)
        for t in writes:
            self.tok[t] = {'w': me, 'r': []}

    def finalize(self):
        self.prefix = {}
        for c in ENGINES:
            pos = {}
            for n, i in enumerate(sorted(self.marked[c])):
                pos[i] = n + 1
            self.prefix[c] = pos

    def val(self, chan, idx):
        if chan.startswith('dl_'):
            return 16 * (idx + 1)
        return self.prefix[chan][idx]

    def emit(self, eng, h, sems):
        for rec in self.ops[eng]:
            for chan in sorted(rec['waits']):
                h.wait_ge(sems[chan], self.val(chan, rec['waits'][chan]))
            ins = rec['fn'](h)
            if rec['dma']:
                ins.then_inc(sems[rec['chan']], 16)
            elif rec['idx'] in self.marked[rec['chan']]:
                ins.then_inc(sems[rec['chan']], 1)


def build_program():
    nc = bass.Bass("TRN2", target_bir_lowering=False)

    dx = nc.dram_tensor("x", [S, D], F32, kind="ExternalInput")
    dwq = nc.dram_tensor("wq_y", [H, 128, 16, 128], F32, kind="ExternalInput")
    dwk = nc.dram_tensor("wk", [D, G * HD], F32, kind="ExternalInput")
    dwv = nc.dram_tensor("wv", [D, G * HD], F32, kind="ExternalInput")
    dwo = nc.dram_tensor("wo", [D, D], F32, kind="ExternalInput")
    dwg = nc.dram_tensor("wg_y", [64, 128, 16, 128], F32, kind="ExternalInput")
    dwu = nc.dram_tensor("wu_y", [64, 128, 16, 128], F32, kind="ExternalInput")
    dwd = nc.dram_tensor("wd_p", [I, D], F32, kind="ExternalInput")
    dbq = nc.dram_tensor("bq_y", [128, H], F32, kind="ExternalInput")
    dbk = nc.dram_tensor("bk_y", [128, G], F32, kind="ExternalInput")
    dbv = nc.dram_tensor("bv_y", [128, H], F32, kind="ExternalInput")
    dbo = nc.dram_tensor("bo_r", [1, D], F32, kind="ExternalInput")
    drope = nc.dram_tensor("rope", [128, 2 * S], F32, kind="ExternalInput")
    dident = nc.dram_tensor("ident", [128, 128], F32, kind="ExternalInput")
    donesc = nc.dram_tensor("ones_c", [128, 1], F32, kind="ExternalInput")
    donesr = nc.dram_tensor("ones_r", [1, 128], F32, kind="ExternalInput")
    deps = nc.dram_tensor("eps_c", [128, 1], F32, kind="ExternalInput")
    dy = nc.dram_tensor("y", [S, D], F32, kind="ExternalOutput")

    es = ExitStack()
    sb = lambda name, shape, dt: es.enter_context(nc.sbuf_tensor(name, shape, dt))
    ps = lambda name, shape, dt: es.enter_context(nc.psum_tensor(name, shape, dt))

    XNT = sb("XNT", [128, 16 * 2048], BF16)
    KT = sb("KT", [128, G * 2048], BF16)
    VSB = sb("VSB", [128, 16 * 256], BF16)
    ACC = sb("ACC", [128, 8192], F32)
    QT = sb("QT", [128, 16 * 512], BF16)
    VAC = sb("VAC", [128, 16 * 512], BF16)
    SCR = sb("SCR", [128, 16 * 512], BF16)
    WSM = sb("WSM", [128, 3 * 2048], BF16)
    XNS = sb("XNS", [128, 2048], BF16)
    RS = sb("RS", [128, 1024], F32)
    SMALL = sb("SMALL", [128, 72], F32)
    BVBF = sb("BVBF", [128, 16], BF16)
    IDENT = sb("IDENT", [128, 128], F32)
    ONESC = sb("ONESC", [128, 1], BF16)
    ONESR = sb("ONESR", [1, 128], F32)
    NORMS = sb("NORMS", [1, 64], F32)
    RROW = sb("RROW", [1, 512], F32)
    BOR = sb("BOR", [1, 2048], F32)

    PP = [ps(f"PP{i}", [128, 1024], F32) for i in range(4)]

    def pphalf(i, half):
        return PP[i][:, half * 512:(half + 1) * 512], ('PB', i, half)

    p = Prog()
    sems = {}

    def mm(dst, lhsT, rhs, reads, wtok, start, stop, tag=""):
        p.op('pe', lambda h, a=dst, b=lhsT, c=rhs, s1=start, s2=stop:
             h.matmul(a, b, c, start=s1, stop=s2), reads=reads, writes=[wtok], tag=tag)

    def transpose(dst, src, reads, wtok, tag=""):
        p.op('pe', lambda h, a=dst, b=src: h.transpose(a, b, IDENT[:, :]),
             reads=list(reads) + ['IDENT'], writes=[wtok], tag=tag)

    def act(fn, reads, writes, tag=""):
        p.op('act', fn, reads=reads, writes=writes, tag=tag)

    def dve(fn, reads, writes, tag=""):
        p.op('dve', fn, reads=reads, writes=writes, tag=tag)

    def copy_any(eng, dst, src, reads, writes, tag=""):
        if eng == 'act':
            p.op('act', lambda h, o=dst, i_=src: h.copy(o, i_), reads=reads, writes=writes, tag=tag)
        else:
            p.op('dve', lambda h, o=dst, i_=src: h.tensor_copy(o, i_), reads=reads, writes=writes, tag=tag)

    def dma(eng, lane, dst, src, reads, writes, tag=""):
        p.op(eng, lambda h, a=dst, b=src: h.dma_start(a, b),
             reads=reads, writes=writes, dma=True, lane=lane, tag=tag)

    def xnt_toks(dc_list, s0, s1):
        t0, t1 = s0 // 128, (s1 + 127) // 128
        return [('XNT', dc, t) for dc in dc_list for t in range(t0, t1)]

    def xnt_ap(dc, s0, s1):
        return XNT[:, dc * 2048 + s0: dc * 2048 + s1]

    def acc_toks(c0, c1):
        return [('ACC', i) for i in range(c0 // 512, (c1 + 511) // 512)]

    def reg_toks(name, c0, c1):
        return [(name, i) for i in range(c0 // 512, (c1 + 511) // 512)]

    QTF = QT.ap().bitcast(F32)
    XNSF = XNS.ap().bitcast(F32)
    VACF = VAC.ap().bitcast(F32)
    SCRF = SCR.ap().bitcast(F32)
    RSBF = RS.ap().bitcast(BF16)

    dma('sync', 'cbq', SMALL[:, 0:16], dbq[:, :], [], [('SM', 'bq')])
    dma('sync', 'cbk', SMALL[:, 16:18], dbk[:, :], [], [('SM', 'bk')])
    dma('sync', 'ceps', SMALL[:, 34:35], deps[:, :], [], [('SM', 'eps')])
    dma('sync', 'cid', IDENT[:, :], dident[:, :], [], ['IDENT'])
    dma('sync', 'conr', ONESR[:, :], donesr[:, :], [], ['ONESR'])
    dma('sync', 'cbor', BOR[:, :], dbo[:, :], [], ['BOR'])
    dma('pool', 'conc', ONESC[:, :], donesc[:, :], [], ['ONESC'])
    dma('pool', 'cbv', BVBF[:, :], dbv[:, :], [], ['BVBF'])
    dma('pool', 'wsm0', WSM[:, 0:4096].rearrange("p (c f) -> p c f", c=16),
        dwk.ap().rearrange("(c p) f -> p c f", p=128), [], [('WSM', 0), ('WSM', 1)], tag="ldwk")
    dma('sync', 'crope', ACC[:, 0:4096], drope[:, :], [], acc_toks(0, 4096))

    def emit_qnorm(hh):
        qsl = 2 + (hh % 2)
        prow = PP[2][0:1, 0:512]
        mm(prow, ONESC[:, :], XNS[:, qsl * 512:(qsl + 1) * 512],
           ['ONESC', ('XNS', qsl)], ('PB', 2, 0), start=True, stop=True, tag=f"qnorm{hh}")
        dve(lambda h, o=NORMS[0:1, hh:hh + 1], i_=prow: h.reduce_max(o, i_, AX),
            [('PB', 2, 0)], [('NM', 'q', hh)])

    def emit_qbuild_head(q, h_):
        qs0 = q * QL
        if q == 0:
            cosl, sinl = ACC[:, 0:512], ACC[:, 2048:2560]
            ctoks = acc_toks(0, 512) + acc_toks(2048, 2560)
        else:
            cosl, sinl = XNSF[:, 0:512], XNSF[:, 512:1024]
            ctoks = [('XNS', i) for i in range(4)]
        sl = (h_ + 2) % 3
        wtok = ('WSM', sl)
        dma('pool', f"wsm{sl}", WSM[:, sl * 2048:(sl + 1) * 2048].rearrange("p (c f) -> p c f", c=16),
            dwq[h_], [], [wtok], tag=f"ldwq{q}_{h_}")
        pb, btok = pphalf(0, h_ % 2)
        for dc in range(16):
            mm(pb, WSM[:, sl * 2048 + dc * 128: sl * 2048 + (dc + 1) * 128],
               xnt_ap(dc, qs0, qs0 + 512),
               [wtok] + xnt_toks([dc], qs0, qs0 + 512),
               btok, start=(dc == 0), stop=(dc == 15), tag=f"q{q}_{h_}")
        act(lambda h, o=RS[:, 0:512], i_=pb, b=SMALL[:, h_:h_ + 1]:
            h.activation(o, i_, AF.Identity, bias=b), [btok, ('SM', 'bq')], ['RS'])
        dve(lambda h, o=RS[0:64, 512:1024], a=RS[64:128, 0:512], b=sinl[64:128, :]:
            h.tensor_tensor(o, a, b, OP.mult), ['RS'] + ctoks, ['RS'])
        dve(lambda h, o=RS[64:128, 512:1024], a=RS[0:64, 0:512], b=sinl[0:64, :]:
            h.tensor_tensor(o, a, b, OP.mult), ['RS'] + ctoks, ['RS'])
        dve(lambda h, o=RS[:, 0:512], a=RS[:, 0:512], b=cosl:
            h.tensor_tensor(o, a, b, OP.mult), ['RS'] + ctoks, ['RS'])
        qap = QT[:, h_ * 512:(h_ + 1) * 512]
        qtok = ('QT', h_)
        dve(lambda h, o=qap, a=RS[:, 0:512], b=RS[:, 512:1024]:
            h.tensor_tensor(o, a, b, OP.add), ['RS'], [qtok], tag=f"qrope{q}_{h_}")
        if q == 0:
            qsl = 2 + (h_ % 2)
            dve(lambda h, o=XNS[:, qsl * 512:(qsl + 1) * 512], a=qap:
                h.tensor_tensor(o, a, a, OP.mult), [qtok], [('XNS', qsl)])
            if h_ > 0:
                emit_qnorm(h_ - 1)

    KTF = KT.ap().bitcast(F32)
    VSBF = VSB.ap().bitcast(F32)

    def emit_ln_tile(t):
        sl = t % 4
        if sl == 0:
            xap, xtok, xlane = SCRF[:, 0:2048], reg_toks('SCR', 0, 4096), 'scrh0'
        elif sl == 1:
            xap, xtok, xlane = SCRF[:, 2048:4096], reg_toks('SCR', 4096, 8192), 'scrh1'
        elif sl == 2:
            xap, xtok, xlane = ACC[:, 4096:6144], acc_toks(4096, 6144), 'lnkt'
        else:
            xap, xtok, xlane = ACC[:, 6144:8192], acc_toks(6144, 8192), 'lnvs'
        if sl < 2:
            cap = QTF[:, sl * 2048:(sl + 1) * 2048]
            ctok = reg_toks('QT', sl * 4096, (sl + 1) * 4096)
        else:
            cap = VACF[:, (sl - 2) * 2048:(sl - 1) * 2048]
            ctok = reg_toks('VAC', (sl - 2) * 4096, (sl - 1) * 4096)
        dma('sync', xlane, xap, dx[t * 128:(t + 1) * 128, :], [], xtok, tag=f"ldx{t}")
        c0 = 36 + sl * 9
        s1 = SMALL[:, c0:c0 + 1]
        st1 = ('SM', 's1', sl)
        s2 = SMALL[:, c0 + 1:c0 + 2]
        t2 = ('SM', 's2', sl)
        dve(lambda h, o=s1, i_=xap: h.reduce_sum(o, i_, AX), xtok, [st1])
        dve(lambda h, o=RS[:, 0:1024], i0=xap[:, 0:1024], i1=xap[:, 0:1024], a=s2:
            h.scalar_tensor_tensor(o, i0, 1.0, i1, OP.mult, OP.mult, accum_out=a),
            xtok, ['RS', t2], tag=f"sqa{t}")
        s2b = SMALL[:, c0 + 2:c0 + 3]
        t2b = ('SM', 's2b', sl)
        dve(lambda h, o=RS[:, 0:1024], i0=xap[:, 1024:2048], i1=xap[:, 1024:2048], a=s2b:
            h.scalar_tensor_tensor(o, i0, 1.0, i1, OP.mult, OP.mult, accum_out=a),
            xtok, ['RS', t2b], tag=f"sqb{t}")
        m = SMALL[:, c0 + 3:c0 + 4]
        mt = ('SM', 'm', sl)
        dve(lambda h, o=m, i_=s1: h.tensor_single_scalar(o, i_, 1.0 / D, OP.mult), [st1], [mt])
        v1 = SMALL[:, c0 + 4:c0 + 5]
        vt1 = ('SM', 'v1', sl)
        dve(lambda h, o=v1, a=s2, b=s2b: h.tensor_tensor(o, a, b, OP.add), [t2, t2b], [vt1])
        msq = SMALL[:, c0 + 5:c0 + 6]
        msqt = ('SM', 'msq', sl)
        dve(lambda h, o=msq, i_=m: h.tensor_tensor(o, i_, i_, OP.mult), [mt], [msqt])
        dve(lambda h, o=v1, a=v1, s=1.0 / D, b=msq: h.scalar_tensor_tensor(o, a, s, b, OP.mult, OP.subtract),
            [vt1, msqt], [vt1])
        std = SMALL[:, c0 + 6:c0 + 7]
        stdt = ('SM', 'std', sl)
        act(lambda h, o=std, i_=v1: h.activation(o, i_, AF.Sqrt, bias=SMALL[:, 34:35]),
            [vt1, ('SM', 'eps')], [stdt])
        rstd = SMALL[:, c0 + 7:c0 + 8]
        rt = ('SM', 'rstd', sl)
        dve(lambda h, o=rstd, i_=std: h.reciprocal(o, i_), [stdt], [rt])
        nmr = SMALL[:, c0 + 8:c0 + 9]
        nmrt = ('SM', 'nmr', sl)
        dve(lambda h, o=nmr, i0=m, b=rstd: h.scalar_tensor_tensor(o, i0, -1.0, b, OP.mult, OP.mult),
            [mt, rt], [nmrt])
        act(lambda h, o=cap, i_=xap, sc_=rstd, b=nmr:
            h.activation(o, i_, AF.Identity, bias=b, scale=sc_),
            xtok + [rt, nmrt], ctok, tag=f"ln{t}")
        for grp in range(4):
            pb, btok = pphalf(2 + grp // 2, grp % 2)
            for k in range(4):
                dc = grp * 4 + k
                transpose(pb[:, k * 128:(k + 1) * 128], cap[:, dc * 128:(dc + 1) * 128],
                          ctok, btok, tag=f"tp{t}_{dc}")
            dst = XNT.ap().rearrange("p (c f) -> p c f", f=2048)[:, grp * 4:grp * 4 + 4,
                                                                t * 128:(t + 1) * 128]
            srcv = pb.rearrange("p (c f) -> p c f", f=128)
            copy_any('act', dst, srcv, [btok],
                     [('XNT', grp * 4 + k, t) for k in range(4)], tag=f"ev{t}_{grp}")

    knorm_pending = []

    def emit_knorm_pending():
        while knorm_pending:
            g, sc, ksl = knorm_pending.pop(0)
            prow = PP[2][0:1, 0:512]
            mm(prow, ONESC[:, :], XNS[:, ksl * 512:(ksl + 1) * 512],
               ['ONESC', ('XNS', ksl)], ('PB', 2, 0), start=True, stop=True, tag="knorm")
            nt = ('NM', 'k', g)
            if sc == 0:
                dve(lambda h, o=NORMS[0:1, 32 + g:33 + g], i_=prow: h.reduce_max(o, i_, AX),
                    [('PB', 2, 0)], [nt])
            else:
                dve(lambda h, o=NORMS[0:1, 36:37], i_=prow: h.reduce_max(o, i_, AX),
                    [('PB', 2, 0)], [('NM', 'tmp')])
                dve(lambda h, o=NORMS[0:1, 32 + g:33 + g], a=NORMS[0:1, 32 + g:33 + g],
                    b=NORMS[0:1, 36:37]: h.tensor_tensor(o, a, b, OP.max),
                    [nt, ('NM', 'tmp')], [nt])

    def emit_k_sc(sc):
        emit_knorm_pending()
        for g in range(G):
            pb, btok = pphalf(0, (2 * sc + g) % 2)
            for dc in range(16):
                mm(pb, WSM[:, dc * 256 + g * 128: dc * 256 + (g + 1) * 128],
                   xnt_ap(dc, sc * 512, (sc + 1) * 512),
                   [('WSM', 0), ('WSM', 1)] + xnt_toks([dc], sc * 512, (sc + 1) * 512),
                   btok, start=(dc == 0), stop=(dc == 15), tag=f"k{g}{sc}")
            act(lambda h, o=RS[:, 0:512], i_=pb, b=SMALL[:, 16 + g:17 + g]:
                h.activation(o, i_, AF.Identity, bias=b), [btok, ('SM', 'bk')], ['RS'])
            cosl = ACC[:, sc * 512:(sc + 1) * 512]
            sinl = ACC[:, 2048 + sc * 512: 2048 + (sc + 1) * 512]
            ctoks = acc_toks(sc * 512, (sc + 1) * 512) + acc_toks(2048 + sc * 512,
                                                                  2048 + (sc + 1) * 512)
            dve(lambda h, o=RS[0:64, 512:1024], a=RS[64:128, 0:512], b=sinl[64:128, :]:
                h.tensor_tensor(o, a, b, OP.mult), ['RS'] + ctoks, ['RS'])
            dve(lambda h, o=RS[64:128, 512:1024], a=RS[0:64, 0:512], b=sinl[0:64, :]:
                h.tensor_tensor(o, a, b, OP.mult), ['RS'] + ctoks, ['RS'])
            dve(lambda h, o=RS[:, 0:512], a=RS[:, 0:512], b=cosl:
                h.tensor_tensor(o, a, b, OP.mult), ['RS'] + ctoks, ['RS'])
            kap = KT[:, g * 2048 + sc * 512: g * 2048 + (sc + 1) * 512]
            ktok = ('KT', g * 4 + sc)
            dve(lambda h, o=kap, a=RS[:, 0:512], b=RS[:, 512:1024]:
                h.tensor_tensor(o, a, b, OP.add), ['RS'], [ktok], tag=f"krope{g}{sc}")
            ksl = (2 * sc + g) % 2
            dve(lambda h, o=XNS[:, ksl * 512:(ksl + 1) * 512], a=kap:
                h.tensor_tensor(o, a, a, OP.mult), [ktok], [('XNS', ksl)])
            knorm_pending.append((g, sc, ksl))

    def warm_mm(bank=(1, 0), n=2):
        pb = PP[bank[0]][:, bank[1] * 512: bank[1] * 512 + 64]
        for _ in range(n):
            mm(pb, IDENT[:, :], IDENT[:, 0:64], ['IDENT'], ('PB', bank[0], bank[1]),
               start=True, stop=True, tag="warm")

    for t in range(16):
        emit_ln_tile(t)
        warm_mm((1, 0), 2)
        if t % 4 == 3:
            emit_k_sc(t // 4)
    emit_knorm_pending()
    for h_ in range(H):
        emit_qbuild_head(0, h_)
    emit_qnorm(H - 1)

    dma('pool', 'wsm0', WSM[:, 0:4096].rearrange("p (c f) -> p c f", c=16),
        dwv.ap().rearrange("(c p) f -> p c f", p=128), [], [('WSM', 0), ('WSM', 1)], tag="ldwv")
    for t in range(16):
        pb, btok = pphalf(1, t % 2)
        pv = pb[:, 0:256]
        for dc in range(16):
            mm(pv, xnt_ap(dc, t * 128, (t + 1) * 128), WSM[:, dc * 256:(dc + 1) * 256],
               [('WSM', 0), ('WSM', 1)] + xnt_toks([dc], t * 128, (t + 1) * 128),
               btok, start=(dc == 0), stop=(dc == 15), tag=f"v{t}")
        act(lambda h, o=VSB[:, t * 256:(t + 1) * 256], i_=pv: h.copy(o, i_),
            [btok], [('VSB', t)], tag=f"vev{t}")

    pending_out_prev = []
    emit_out_band_prev = None
    for q in range(NQT):
        qs0 = q * QL

        if q > 0:
            dma('sync', 'ropec', XNSF[:, 0:512], drope[:, qs0:qs0 + 512],
                [], [('XNS', 0), ('XNS', 1)], tag=f"rc{q}")
            dma('sync', 'ropes', XNSF[:, 512:1024], drope[:, 2048 + qs0: 2048 + qs0 + 512],
                [], [('XNS', 2), ('XNS', 3)], tag=f"rs{q}")
            warm_mm((2, 0), 4)
            for h_ in range(H):
                emit_qbuild_head(q, h_)

        if pending_out_prev:
            for band in pending_out_prev:
                emit_out_band_prev(band)
            pending_out_prev = []

        if q == 0:
            nr = [('NM', 'q', hh) for hh in range(H)] + [('NM', 'k', 0), ('NM', 'k', 1)]
            dve(lambda h, o=NORMS[0:1, 40:48], i_=NORMS[0:1, 0:8], s=NORMS[0:1, 32:33]:
                h.tensor_single_scalar(o, i_, s, OP.add), nr, [('NM', 'c')])
            dve(lambda h, o=NORMS[0:1, 48:56], i_=NORMS[0:1, 8:16], s=NORMS[0:1, 33:34]:
                h.tensor_single_scalar(o, i_, s, OP.add), nr + [('NM', 'c')], [('NM', 'c')])
            dve(lambda h, o=NORMS[0:1, 40:56], i_=NORMS[0:1, 40:56]:
                h.tensor_single_scalar(o, i_, -1.6 / 256.0, OP.mult), [('NM', 'c')], [('NM', 'c')])
            pbias = PP[2][:, 512:528]
            mm(pbias, ONESR[:, :], NORMS[0:1, 40:56], ['ONESR', ('NM', 'c')], ('PB', 2, 1),
               start=True, stop=True, tag="cbias")
            act(lambda h, o=SMALL[:, 18:34], i_=pbias: h.copy(o, i_), [('PB', 2, 1)],
                [('SM', 'biasC')])

        wo_dma_recs = []
        for dmq in range(4):
            for half in range(2):
                dst = SCR[:, half * 4096:(half + 1) * 4096].rearrange("p (c f) -> p c f", c=8)
                src = dwo[half * 1024:(half + 1) * 1024, dmq * 512:(dmq + 1) * 512].rearrange(
                    "(c p) f -> p c f", p=128)
                wo_dma_recs.append((dst, src, reg_toks('SCR', half * 4096, (half + 1) * 4096),
                                    f"ldwo{q}_{dmq}_{half}"))

        def emit_finalize(hh):
            sp = hh % 2
            prow_f = PP[3][0:1, sp * 512: sp * 512 + 512]
            pbc = PP[3][:, sp * 512:(sp + 1) * 512]
            pv_f = PP[2][:, sp * 512: sp * 512 + 512]
            dve(lambda h, o=RROW[:, :], i_=prow_f: h.reciprocal(o, i_),
                [('PB', 3, sp)], ['RROW'])
            mm(pbc, ONESR[:, :], RROW[:, :], ['ONESR', 'RROW'], ('PB', 3, sp),
               start=True, stop=True, tag=f"bc{q}_{hh}")
            rtile = ACC[:, 1024:1536]
            act(lambda h, o=rtile, i_=pbc: h.copy(o, i_), [('PB', 3, sp)], acc_toks(1024, 1536))
            dve(lambda h, o=VAC[:, hh * 512:(hh + 1) * 512], a=pv_f, b=rtile:
                h.tensor_tensor(o, a, b, OP.mult), [('PB', 2, sp)] + acc_toks(1024, 1536),
                [('VAC', hh)], tag=f"nv{q}_{hh}")

        for h_ in range(H):
            g = h_ // 8
            vtok = ('PB', 2, h_ % 2)
            pv = PP[2][:, (h_ % 2) * 512:(h_ % 2) * 512 + 512]
            prow = PP[3][0:1, (h_ % 2) * 512:(h_ % 2) * 512 + 512]

            def stage_scores(jp):
                pi = jp % 2
                for u in range(2):
                    j = 2 * jp + u
                    pb, btok = pphalf(pi, u)
                    mm(pb, KT[:, g * 2048 + j * 128: g * 2048 + (j + 1) * 128],
                       QT[:, h_ * 512:(h_ + 1) * 512],
                       [('KT', g * 4 + j // 4), ('QT', h_)], btok,
                       start=True, stop=True, tag=f"s{q}_{h_}_{j}")
                esl = jp % 2
                eap = XNS[:, esl * 1024:(esl + 1) * 1024]
                etoks = [('XNS', 2 * esl), ('XNS', 2 * esl + 1)]
                act(lambda h, o=eap, i_=PP[pi][:, :], b=SMALL[:, 18 + h_:19 + h_]:
                    h.activation(o, i_, AF.Exp, bias=b, scale=1.0 / 128.0),
                    [('PB', pi, 0), ('PB', pi, 1), ('SM', 'biasC')], etoks, tag=f"e{q}_{h_}_{jp}")

            def stage_consume(jp):
                esl = jp % 2
                eap = XNS[:, esl * 1024:(esl + 1) * 1024]
                etoks = [('XNS', 2 * esl), ('XNS', 2 * esl + 1)]
                for u in range(2):
                    j = 2 * jp + u
                    mm(pv, VSB[:, j * 256 + g * 128: j * 256 + (g + 1) * 128],
                       eap[:, u * 512:(u + 1) * 512],
                       [('VSB', j)] + etoks, vtok,
                       start=(jp == 0 and u == 0), stop=(jp == 7 and u == 1), tag=f"va{q}_{h_}_{j}")
                for u in range(2):
                    mm(prow, ONESC[:, :], eap[:, u * 512:(u + 1) * 512],
                       ['ONESC'] + etoks, ('PB', 3, h_ % 2),
                       start=(jp == 0 and u == 0), stop=(jp == 7 and u == 1), tag=f"sm{q}_{h_}_{jp}")

            for jp in range(9):
                if jp < 8:
                    stage_scores(jp)
                if jp == 2 and h_ > 0:
                    emit_finalize(h_ - 1)
                if jp >= 1:
                    stage_consume(jp - 1)
        emit_finalize(H - 1)

        for dmq in range(4):
            warm_mm((2, 0), 2)
            for half in range(2):
                dst_, src_, wtoks, tg = wo_dma_recs[dmq * 2 + half]
                dma('pool', f"wo{half}", dst_, src_, [], wtoks, tag=tg)
            for half in range(2):
                wtoks = reg_toks('SCR', half * 4096, (half + 1) * 4096)
                for qsub in range(4):
                    pb, btok = pphalf(qsub // 2, qsub % 2)
                    for h_ in range(8 * half, 8 * half + 8):
                        rhs = SCR[:, half * 4096 + (h_ % 8) * 512:
                                  half * 4096 + (h_ % 8 + 1) * 512]
                        mm(pb, VAC[:, h_ * 512 + qsub * 128: h_ * 512 + qsub * 128 + 128], rhs,
                           [('VAC', h_)] + wtoks, btok,
                           start=(h_ == 0), stop=(h_ == 15), tag=f"o{q}_{dmq}_{qsub}")
            for qsub in range(4):
                pb, btok = pphalf(qsub // 2, qsub % 2)
                accap = ACC[:, qsub * 2048 + dmq * 512: qsub * 2048 + (dmq + 1) * 512]
                atoks = acc_toks(qsub * 2048 + dmq * 512, qsub * 2048 + (dmq + 1) * 512)
                act(lambda h, o=accap, i_=pb: h.copy(o, i_), [btok], atoks,
                    tag=f"oev{q}_{dmq}_{qsub}")
            if q == 0:
                prow = PP[3][0:1, 0:512]
                for h_ in range(16):
                    half = h_ // 8
                    wtoks = reg_toks('SCR', half * 4096, (half + 1) * 4096)
                    rhs = SCR[:, half * 4096 + (h_ % 8) * 512: half * 4096 + (h_ % 8 + 1) * 512]
                    mm(prow, BVBF[:, h_:h_ + 1], rhs, ['BVBF'] + wtoks, ('PB', 3, 0),
                       start=(h_ == 0), stop=(h_ == 15), tag=f"r{dmq}")
                dve(lambda h, o=BOR[0:1, dmq * 512:(dmq + 1) * 512],
                    a=BOR[0:1, dmq * 512:(dmq + 1) * 512], b=prow:
                    h.tensor_tensor(o, a, b, OP.add), [('PB', 3, 0), 'BOR'], ['BOR'],
                    tag=f"bor{dmq}")

        def emit_sparsify(b_, stage):
            tl = [4 * b_ + j for j in range(4)]
            htoks = [('SCR', x) for x in tl]
            sB = lambda k2: RSBF[:, k2 * 512:(k2 + 1) * 512]
            aB = lambda j2: XNS[:, j2 * 512:(j2 + 1) * 512]
            xt = [('XNS', j) for j in range(4)]
            if stage == 0:
                for j in range(4):
                    dve(lambda h, o=XNS.ap().bitcast(U16)[:, j * 512:(j + 1) * 512],
                        a=SCR.ap().bitcast(U16)[:, tl[j] * 512:(tl[j] + 1) * 512]:
                        h.tensor_single_scalar(o, a, 0x7FFF, OP.bitwise_and),
                        [htoks[j]], [('XNS', j)], tag=f"abs{b_}_{j}")
            elif stage == 1:
                dve(lambda h, o=sB(0), a=aB(0), b=aB(1): h.tensor_tensor(o, a, b, OP.max),
                    xt, ['RS'])
                dve(lambda h, o=sB(1), a=aB(0), b=aB(1): h.tensor_tensor(o, a, b, OP.min),
                    xt + ['RS'], ['RS'])
                dve(lambda h, o=sB(2), a=aB(2), b=aB(3): h.tensor_tensor(o, a, b, OP.max),
                    xt + ['RS'], ['RS'])
                dve(lambda h, o=sB(3), a=aB(2), b=aB(3): h.tensor_tensor(o, a, b, OP.min),
                    xt + ['RS'], ['RS'])
            elif stage == 2:
                dve(lambda h, o=sB(0), a=sB(0), b=sB(2): h.tensor_tensor(o, a, b, OP.min),
                    ['RS'], ['RS'])
                dve(lambda h, o=sB(1), a=sB(1), b=sB(3): h.tensor_tensor(o, a, b, OP.max),
                    ['RS'], ['RS'])
                dve(lambda h, o=sB(0), a=sB(0), b=sB(1): h.tensor_tensor(o, a, b, OP.max),
                    ['RS'], ['RS'])
            else:
                for j in range(4):
                    dve(lambda h, o=aB(j), a=aB(j), b=sB(0): h.tensor_tensor(o, a, b, OP.is_ge),
                        ['RS', ('XNS', j)], [('XNS', j)], tag=f"ge{j}")
                for j in range(4):
                    dve(lambda h, o=SCR[:, tl[j] * 512:(tl[j] + 1) * 512],
                        a=SCR[:, tl[j] * 512:(tl[j] + 1) * 512], b=aB(j):
                        h.tensor_tensor(o, a, b, OP.mult), [('XNS', j), htoks[j]], [htoks[j]],
                        tag=f"msk{j}")

        for iq in range(4):
            for it in range(16):
                T = iq * 16 + it
                gsl = it % 4
                gtoks = [('QT', 4 * gsl + k) for k in range(4)]
                utoks = [('VAC', 4 * gsl + k) for k in range(4)]
                dma('pool', f"qtg{gsl}", QT[:, gsl * 2048:(gsl + 1) * 2048].rearrange("p (c f) -> p c f", c=16),
                    dwg[T], [], gtoks, tag=f"ldg{q}_{T}")
                dma('pool', f"vacu{gsl}", VAC[:, gsl * 2048:(gsl + 1) * 2048].rearrange("p (c f) -> p c f", c=16),
                    dwu[T], [], utoks, tag=f"ldu{q}_{T}")
                pg, gt = pphalf(0, it % 2)
                pu, ut = pphalf(1, it % 2)
                for dc in range(16):
                    mm(pg, QT[:, gsl * 2048 + dc * 128: gsl * 2048 + (dc + 1) * 128],
                       xnt_ap(dc, qs0, qs0 + 512),
                       gtoks + xnt_toks([dc], qs0, qs0 + 512), gt,
                       start=(dc == 0), stop=(dc == 15), tag=f"g{q}_{T}")
                for dc in range(16):
                    mm(pu, VAC[:, gsl * 2048 + dc * 128: gsl * 2048 + (dc + 1) * 128],
                       xnt_ap(dc, qs0, qs0 + 512),
                       utoks + xnt_toks([dc], qs0, qs0 + 512), ut,
                       start=(dc == 0), stop=(dc == 15), tag=f"u{q}_{T}")
                htok = ('SCR', it)
                act(lambda h, o=SCR[:, it * 512:(it + 1) * 512], i_=pg: h.activation(o, i_, AF.Silu),
                    [gt], [htok], tag=f"si{q}_{T}")
                dve(lambda h, o=SCR[:, it * 512:(it + 1) * 512],
                    a=SCR[:, it * 512:(it + 1) * 512], b=pu:
                    h.tensor_tensor(o, a, b, OP.mult), [htok, ut], [htok], tag=f"h{q}_{T}")
                for b_ in range(4):
                    s_ = it - (4 * b_ + 3)
                    if 0 <= s_ <= 3:
                        emit_sparsify(b_, s_)
            for b_ in range(4):
                for s_ in range(4):
                    if 4 * b_ + 3 + s_ > 15:
                        emit_sparsify(b_, s_)
            for b_ in []:
                tl = [4 * j + b_ for j in range(4)]
                htoks = [('SCR', x) for x in tl]
                for j in range(4):
                    dve(lambda h, o=XNS.ap().bitcast(U16)[:, j * 512:(j + 1) * 512],
                        a=SCR.ap().bitcast(U16)[:, tl[j] * 512:(tl[j] + 1) * 512]:
                        h.tensor_single_scalar(o, a, 0x7FFF, OP.bitwise_and),
                        [htoks[j]], [('XNS', j)], tag=f"abs{b_}_{j}")
                sB = lambda k: RSBF[:, k * 512:(k + 1) * 512]
                aB = lambda j: XNS[:, j * 512:(j + 1) * 512]
                xt = [('XNS', j) for j in range(4)]
                dve(lambda h, o=sB(0), a=aB(0), b=aB(1): h.tensor_tensor(o, a, b, OP.max),
                    xt, ['RS'])
                dve(lambda h, o=sB(1), a=aB(0), b=aB(1): h.tensor_tensor(o, a, b, OP.min),
                    xt + ['RS'], ['RS'])
                dve(lambda h, o=sB(2), a=aB(2), b=aB(3): h.tensor_tensor(o, a, b, OP.max),
                    xt + ['RS'], ['RS'])
                dve(lambda h, o=sB(3), a=aB(2), b=aB(3): h.tensor_tensor(o, a, b, OP.min),
                    xt + ['RS'], ['RS'])
                dve(lambda h, o=sB(0), a=sB(0), b=sB(2): h.tensor_tensor(o, a, b, OP.min),
                    ['RS'], ['RS'])
                dve(lambda h, o=sB(1), a=sB(1), b=sB(3): h.tensor_tensor(o, a, b, OP.max),
                    ['RS'], ['RS'])
                dve(lambda h, o=sB(0), a=sB(0), b=sB(1): h.tensor_tensor(o, a, b, OP.max),
                    ['RS'], ['RS'])
                for j in range(4):
                    dve(lambda h, o=aB(j), a=aB(j), b=sB(0): h.tensor_tensor(o, a, b, OP.is_ge),
                        ['RS', ('XNS', j)], [('XNS', j)], tag=f"ge{j}")
                for j in range(4):
                    dve(lambda h, o=SCR[:, tl[j] * 512:(tl[j] + 1) * 512],
                        a=SCR[:, tl[j] * 512:(tl[j] + 1) * 512], b=aB(j):
                        h.tensor_tensor(o, a, b, OP.mult), [('XNS', j), htoks[j]], [htoks[j]],
                        tag=f"msk{j}")
            for ph in range(2):
                warm_mm((0, 0), 2)
                for it in range(16):
                    T = iq * 16 + it
                    wsl = (it // 2) % 3
                    wtok = ('WSM', wsl)
                    if it % 2 == 0:
                        dma('pool', f"wsm{wsl}",
                            WSM[:, wsl * 2048:(wsl + 1) * 2048].rearrange("p (t f) -> p t f", t=2),
                            dwd[T * 128:(T + 2) * 128,
                                ph * 1024:(ph + 1) * 1024].rearrange("(t p) f -> p t f", p=128),
                            [], [wtok], tag=f"ldd{q}_{iq}_{ph}_{it}")
                    off = (it % 2) * 1024
                    for band in range(4):
                        for c in range(2):
                            pb, btok = pphalf(band, c)
                            mm(pb, SCR[:, it * 512 + band * 128: it * 512 + band * 128 + 128],
                               WSM[:, wsl * 2048 + off + c * 512: wsl * 2048 + off + (c + 1) * 512],
                               [('SCR', it), wtok], btok,
                               start=(it == 0), stop=(it == 15),
                               tag=f"d{q}_{iq}_{ph}_{it}_{band}{c}")
                for band in range(4):
                    for c in range(2):
                        pb, btok = pphalf(band, c)
                        dmc = 2 * ph + c
                        accap = ACC[:, band * 2048 + dmc * 512: band * 2048 + (dmc + 1) * 512]
                        atoks = acc_toks(band * 2048 + dmc * 512, band * 2048 + (dmc + 1) * 512)
                        dve(lambda h, o=accap, a=accap, b=pb: h.tensor_tensor(o, a, b, OP.add),
                            [btok] + atoks, atoks, tag=f"df{q}_{iq}_{ph}_{band}{c}")

        for c4 in range(4):
            pbc = PP[3][:, 512:1024]
            mm(pbc, ONESR[:, :], BOR[0:1, c4 * 512:(c4 + 1) * 512], ['ONESR', 'BOR'],
               ('PB', 3, 1), start=True, stop=True, tag=f"bob{q}_{c4}")
            act(lambda h, o=VACF[:, c4 * 512:(c4 + 1) * 512], i_=pbc: h.copy(o, i_),
                [('PB', 3, 1)], reg_toks('VAC', c4 * 1024, (c4 + 1) * 1024), tag=f"bobe{q}_{c4}")
        bobtoks = reg_toks('VAC', 0, 4096)

        def emit_out_band(band, qs0=qs0, q=q):
            sl = band % 2
            xap = SCRF[:, sl * 2048:(sl + 1) * 2048]
            xtok = reg_toks('SCR', sl * 4096, (sl + 1) * 4096)
            r0 = qs0 + band * 128
            dma('sync', f"scrh{sl}", xap, dx[r0:r0 + 128, :], [], xtok, tag=f"ox{q}_{band}")
            atoks = acc_toks(band * 2048, (band + 1) * 2048)
            p.op('pool', lambda h, o=xap, a=xap, b=ACC[:, band * 2048:(band + 1) * 2048]:
                 h.tensor_tensor(o, a, b, OP.add), reads=xtok + atoks, writes=xtok,
                 tag=f"oa{q}_{band}")
            p.op('pool', lambda h, o=xap, a=xap, b=VACF[:, 0:2048]:
                 h.tensor_tensor(o, a, b, OP.add), reads=xtok + bobtoks, writes=xtok,
                 tag=f"ob{q}_{band}")
            dma('sync', f"scrh{sl}", dy[r0:r0 + 128, :], xap, xtok, [('Y', q, band)], tag=f"oy{q}_{band}")

        pending_out_prev = [0, 1, 2, 3]
        emit_out_band_prev = emit_out_band

    if pending_out_prev:
        for band in pending_out_prev:
            emit_out_band_prev(band)

    p.finalize()
    for c in list(ENGINES) + p.lanes:
        sems[c] = es.enter_context(nc.semaphore("sem_" + c.replace('dl_', '')))

    with nc.Block() as block:
        @block.tensor
        def _(t):
            p.emit('pe', t, sems)

        @block.scalar
        def _(s_):
            p.emit('act', s_, sems)

        @block.vector
        def _(v):
            p.emit('dve', v, sems)

        @block.gpsimd
        def _(g_):
            p.emit('pool', g_, sems)

        @block.sync
        def _(sy):
            p.emit('sync', sy, sems)
            for c in p.lanes:
                sy.wait_ge(sems[c], 16 * p.cnt[c])

    return nc


_NC = None


def _host_prep(wq, wk, wv, wo, bq, bk, bv, bo, wg, wu, wd):
    f = np.float32
    wq_y = np.ascontiguousarray(wq.reshape(16, 128, 16, 128).transpose(2, 1, 0, 3))
    iqv, bv_, jv, pv = np.meshgrid(np.arange(4), np.arange(4), np.arange(4),
                                   np.arange(128), indexing='ij')
    perm = np.empty(I, dtype=np.int64)
    perm[(iqv * 2048 + bv_ * 512 + jv * 128 + pv).ravel()] = \
        (4 * (512 * iqv + 128 * bv_ + pv) + jv).ravel()
    wg_y = np.ascontiguousarray(wg[:, perm].reshape(16, 128, 64, 128).transpose(2, 1, 0, 3))
    wu_y = np.ascontiguousarray(wu[:, perm].reshape(16, 128, 64, 128).transpose(2, 1, 0, 3))
    wd_p = np.ascontiguousarray(wd[perm, :])
    bq_y = np.ascontiguousarray(bq.reshape(16, 128).T)
    bk_y = np.ascontiguousarray(bk.reshape(2, 128).T)
    bv2 = bv.reshape(2, 128)
    bv_y = np.ascontiguousarray(np.stack([bv2[h // 8] for h in range(16)], axis=1))
    bo_r = np.ascontiguousarray(bo.reshape(1, D))
    inv = 1.0 / 10000 ** (np.arange(0, HD, 2, dtype=np.float64) / HD)
    fr = np.concatenate([inv, inv])
    ang = fr[:, None] * np.arange(S, dtype=np.float64)[None, :]
    cos_t = np.cos(ang)
    sinx = np.sin(ang)
    sinx[0:64, :] = -sinx[0:64, :]
    sinx = np.roll(sinx, -64, axis=0)
    rope = np.concatenate([cos_t, sinx], axis=1).astype(f)
    return dict(wq_y=wq_y, wk=np.ascontiguousarray(wk), wv=np.ascontiguousarray(wv),
                wo=np.ascontiguousarray(wo), wg_y=wg_y, wu_y=wu_y, wd_p=wd_p,
                bq_y=bq_y, bk_y=bk_y, bv_y=bv_y, bo_r=bo_r, rope=rope,
                ident=np.eye(128, dtype=f), ones_c=np.ones((128, 1), dtype=f),
                ones_r=np.ones((1, 128), dtype=f), eps_c=np.full((128, 1), EPS, dtype=f))


def kernel(L0_attn_x, L0_attn_wq, L0_attn_bq, L0_attn_wk, L0_attn_bk,
           L0_attn_wv, L0_attn_bv, L0_attn_wo, L0_attn_bo,
           L0_ffn_w_gate, L0_ffn_w_up, L0_ffn_w_down):
    global _NC
    a32 = lambda v: np.asarray(v, dtype=np.float32)
    x = a32(L0_attn_x)
    common = _host_prep(a32(L0_attn_wq), a32(L0_attn_wk), a32(L0_attn_wv), a32(L0_attn_wo),
                        a32(L0_attn_bq), a32(L0_attn_bk), a32(L0_attn_bv), a32(L0_attn_bo),
                        a32(L0_ffn_w_gate), a32(L0_ffn_w_up), a32(L0_ffn_w_down))
    if _NC is None:
        _NC = build_program()
    in_maps = [dict(common, x=np.ascontiguousarray(x[b])) for b in range(8)]
    res = run_bass_kernel_spmd(_NC, in_maps, core_ids=list(range(8)))
    return np.stack([np.asarray(r["y"], dtype=np.float32) for r in res.results], axis=0)


if __name__ == "__main__":
    nc = build_program()
    print("built ok; sbuf_remaining:", nc.sbuf_bytes_remaining)
